# version 18
# baseline (speedup 1.0000x reference)
import contextlib
import numpy as np
import concourse.bass as bass
import concourse.mybir as mybir
from concourse.bass_utils import run_bass_kernel_spmd

F32 = mybir.dt.float32
I32 = mybir.dt.int32
BF16 = mybir.dt.bfloat16
FP8 = mybir.dt.float8e4
AF = mybir.ActivationFunctionType
ALU = mybir.AluOpType
AX = mybir.AxisListType

P = 128
L = 2048
D = 1024
DI = 4096
NST = 16
LC = 256
NCH = L // LC
NDB = D // P
EPS = 1e-5

C_EPS, C_NHALF, C_PHALF, C_NHALFSC, C_127 = 0, 1, 2, 3, 4


class Prog:
    STREAM = {"t": "t", "v": "v", "s": "s", "q": "q", "g": "g", "r": "s"}
    QPOOL = {"q": 16, "g": 8, "r": 4}

    def __init__(self, nc, es):
        self.sems = {e: es.enter_context(nc.semaphore(f"sem_{e}"))
                     for e in ("t", "v", "s")}
        for e, K in self.QPOOL.items():
            for p in range(K):
                self.sems[f"{e}{p}"] = es.enter_context(
                    nc.semaphore(f"sem_{e}{p}"))
        self.cnt = {e: 0 for e in ("t", "v", "s")}
        self.use = {e: 0 for e in self.QPOOL}
        self.ops = {e: [] for e in ("t", "v", "s", "q", "g")}
        self.wtok = {}
        self.rtok = {}
        self.waited = {}
        self.stash_q = []
        self.stashing = False

    def drain(self, n=None):
        take = len(self.stash_q) if n is None else min(n, len(self.stash_q))
        items = self.stash_q[:take]
        del self.stash_q[:take]
        for eng, emit, reads, writes in items:
            self.op(eng, emit, reads, writes)

    def op(self, eng, emit, reads=(), writes=()):
        if self.stashing:
            self.stash_q.append((eng, emit, reads, writes))
            return None
        waits = {}

        def addw(tok):
            if tok is None:
                return
            sn, v = tok
            if sn == "t" and eng == "t":
                return
            if v > waits.get(sn, 0):
                waits[sn] = v

        for b in reads:
            addw(self.wtok.get(b))
        for b in writes:
            addw(self.wtok.get(b))
            for sn, v in self.rtok.get(b, {}).items():
                addw((sn, v))
        if eng in self.QPOOL:
            K = self.QPOOL[eng]
            i = self.use[eng]
            self.use[eng] += 1
            sn = f"{eng}{i % K}"
            n = i // K + 1
            if n > 1 and waits.get(sn, 0) < 16 * (n - 1):
                waits[sn] = 16 * (n - 1)
            tok = (sn, 16 * n)
        else:
            self.cnt[eng] += 1
            tok = (eng, self.cnt[eng])
        wcache = self.waited.setdefault(eng, {})
        final = {}
        for sn, v in waits.items():
            if wcache.get(sn, 0) >= v:
                continue
            final[sn] = v
            wcache[sn] = v
        self.ops[self.STREAM[eng]].append((final, emit, tok))
        for b in writes:
            self.wtok[b] = tok
            self.rtok[b] = {}
        for b in reads:
            d = self.rtok.setdefault(b, {})
            if tok[1] > d.get(tok[0], 0):
                d[tok[0]] = tok[1]
        return tok

    def runner(self, eng):
        sems, ops = self.sems, self.ops[eng]
        finals = []
        if eng == "q":
            for e, K in self.QPOOL.items():
                for p in range(K):
                    uses = (self.use[e] - p + K - 1) // K
                    if uses > 0:
                        finals.append((f"{e}{p}", 16 * uses))

        def run(h):
            for waits, emit, tok in ops:
                for sn in sorted(waits):
                    h.wait_ge(sems[sn], waits[sn])
                inst = emit(h)
                if inst is not None:
                    inst.then_inc(sems[tok[0]], 16 if len(tok[0]) > 1 else 1)
            for sn, v in finals:
                h.wait_ge(sems[sn], v)
        return run


def build_nc():
    nc = bass.Bass("TRN2", target_bir_lowering=False)

    def inp(name, shape, dtype=F32):
        return nc.dram_tensor(name, shape, dtype, kind="ExternalInput").ap()

    x_d = inp("x", [L, D])
    ipwT_d = inp("ipwT", [D, 2 * D])
    ipb_d = inp("ipb_c", [P, 16])
    convw_d = inp("convw_c", [P, 32])
    convb_d = inp("convb_c", [P, 8])
    bcwT_d = inp("bcwT", [D, 32])
    bcb_d = inp("bcb_c", [32, 1])
    dtwT_d = inp("dtwT", [D, D])
    dtb_d = inp("dtb_c", [P, 8])
    alog_d = inp("alog_c", [P, P])
    dcol_d = inp("d_c", [P, 8])
    outwT_d = inp("outwT", [D, D])
    outb_d = inp("outb_r", [P, D])
    lng_d = inp("lng_r", [P, D])
    lnb_d = inp("lnb_r", [P, D])
    wg_d = inp("wg", [DI, D])
    wu_d = inp("wu", [DI, D])
    wd_d = inp("wd", [D, DI])
    ident_d = inp("ident", [P, P])
    ones_d = inp("onescol", [P, 1])
    consts_d = inp("consts", [P, 8])

    out_d = nc.dram_tensor("out", [L, D], F32, kind="ExternalOutput").ap()

    swgT_d = nc.dram_tensor("swgT", [D, DI], FP8, kind="Internal").ap()
    swuT_d = nc.dram_tensor("swuT", [D, DI], FP8, kind="Internal").ap()
    swdT_d = nc.dram_tensor("swdT", [DI, D], FP8, kind="Internal").ap()
    sbc_d = nc.dram_tensor("sbc", [32, LC], F32, kind="Internal").ap()
    ssc_d = nc.dram_tensor("ssc", [1, 8], F32, kind="Internal").ap()
    sip_d = nc.dram_tensor("sip", [D, 2 * D], BF16, kind="Internal").ap()
    sdt_d = nc.dram_tensor("sdt", [D, D], BF16, kind="Internal").ap()
    sout_d = nc.dram_tensor("sout", [D, D], BF16, kind="Internal").ap()

    es = contextlib.ExitStack()
    with es:
        def sb(name, shape, dtype=F32):
            return es.enter_context(nc.sbuf_tensor(name, shape, dtype))

        ipb = sb("ipb", [P, 16]); convw = sb("convw", [P, 32])
        convb = sb("convb", [P, 8]); bcb = sb("bcb", [32, 1])
        dtb = sb("dtb", [P, 8]); alog = sb("alog", [P, P])
        a_t = sb("a_t", [P, P]); dcol = sb("dcol", [P, 8])
        outb = sb("outb", [P, D]); lng = sb("lng", [P, D]); lnb = sb("lnb", [P, D])
        ident = sb("identsb", [P, P]); ones = sb("onessb", [P, 1])
        consts = sb("constssb", [P, 8])
        bcwT = sb("bcwTsb", [P, NDB * 32], BF16)
        carry = sb("carry", [P, P])
        mcg = sb("mcg", [P, 1]); mcu = sb("mcu", [P, 1]); mcd = sb("mcd", [P, 1])
        swg_c = sb("swg_c", [P, 1]); swu_c = sb("swu_c", [P, 1])
        swd_c = sb("swd_c", [P, 1])
        qsc = sb("qsc", [1, 8])
        accq = sb("accq", [P, 32])

        st_sum = sb("st_sum", [P, 2]); st_ssq = sb("st_ssq", [P, 2])
        st_mean = sb("st_mean", [P, 2]); st_var = sb("st_var", [P, 2])
        st_rstd = sb("st_rstd", [P, 2]); st_nmr = sb("st_nmr", [P, 2])
        st_tmp = sb("st_tmp", [P, 2])
        s2_sum = sb("s2_sum", [P, 2]); s2_ssq = sb("s2_ssq", [P, 2])
        s2_mean = sb("s2_mean", [P, 2]); s2_var = sb("s2_var", [P, 2])
        s2_rstd = sb("s2_rstd", [P, 2]); s2_nmr = sb("s2_nmr", [P, 2])
        s2_tmp = sb("s2_tmp", [P, 2])
        fx_mx = sb("fx_mx", [P, 2]); fx_sx = sb("fx_sx", [P, 2])
        fx_sxi = sb("fx_sxi", [P, 2])
        fz_mx = sb("fz_mx", [P, 2]); fz_sz = sb("fz_sz", [P, 2])
        fz_szi = sb("fz_szi", [P, 2])
        sc_g = sb("sc_g", [P, 2]); sc_u = sb("sc_u", [P, 2]); sc_d = sb("sc_d", [P, 2])

        xbuf = [[sb(f"xbuf{par}_{i}", [P, D]) for i in range(2)]
                for par in range(3)]
        xzn = [sb(f"xzn{i}", [P, D]) for i in range(2)]
        xnT = sb("xnT", [P, NDB * LC], BF16)
        xpad = sb("xpad", [P, NDB * (LC + 4)])
        xcv = sb("xcv", [P, NDB * LC])
        sgt = sb("sgt", [P, NDB * LC])
        dtt = sb("dtt", [P, NDB * LC])
        bct = sb("bct", [32, LC])
        bcc = [sb(f"bcc{i}", [P, LC]) for i in range(4)]
        pool6 = [sb(f"pool6_{i}", [P, LC]) for i in range(13)]
        wstr = [sb(f"wstr{i}", [P, D], BF16) for i in range(2)]

        zbuf2 = [sb(f"zbuf2_{i}", [P, DI]) for i in range(2)]
        upf = [sb(f"upf{i}", [P, DI]) for i in range(2)]
        rscr = sb("rscr", [P, 2048])
        ffci = sb("ffci", [P, 2048], I32)
        ffci2 = sb("ffci2", [P, LC], I32)
        xqT = sb("xqT", [P, NDB * 256], BF16)
        zqT = sb("zqT", [P, DI], BF16)
        wsg = [sb(f"wsg{i}", [P, 2048], FP8) for i in range(2)]
        wsd = [sb(f"wsd{i}", [P, 2048], FP8) for i in range(2)]

        pb = [es.enter_context(nc.psum_tensor(f"pb{i}", [P, 512], F32))
              for i in range(8)]

        prog = Prog(nc, es)
        OP = prog.op

        def dma(queue, dst, src, reads, writes):
            return OP(queue, lambda h, d=dst, s=src: h.dma_start(out=d, in_=s),
                      reads=reads, writes=writes)

        for (t, d_, nm) in [(ipb, ipb_d, "ipb"), (convw, convw_d, "convw"),
                            (convb, convb_d, "convb"), (bcb, bcb_d, "bcb"),
                            (dtb, dtb_d, "dtb"), (alog, alog_d, "alog"),
                            (dcol, dcol_d, "dcol"), (outb, outb_d, "outb"),
                            (lng, lng_d, "lng"), (lnb, lnb_d, "lnb"),
                            (ident, ident_d, "ident"), (ones, ones_d, "ones"),
                            (consts, consts_d, "consts")]:
            dma("q", t[:], d_, reads=(), writes=(nm,))
        for k in range(NDB):
            dma("g", bcwT[:, k * 32:(k + 1) * 32], bcwT_d[k * P:(k + 1) * P, :],
                reads=(), writes=("bcwT",))
        OP("s", lambda h: h.activation(out=a_t[:], in_=alog[:], func=AF.Exp),
           reads=("alog",), writes=("a_t",))
        OP("v", lambda h: h.tensor_scalar(out=a_t[:], in0=a_t[:], scalar1=-1.0,
                                          scalar2=None, op0=ALU.mult),
           reads=("a_t",), writes=("a_t",))
        OP("v", lambda h: h.memset(carry[:], 0.0), reads=(), writes=("carry",))

        for (w_d, s_d, cols, tag) in ((ipwT_d, sip_d, 2 * D, "ip"),
                                      (dtwT_d, sdt_d, D, "dt"),
                                      (outwT_d, sout_d, D, "out")):
            for k in range(NDB):
                for half in range(cols // D):
                    wb = wstr[k % 2]
                    dma("g", wb[:], w_d[k * P:(k + 1) * P,
                                        half * D:(half + 1) * D],
                        reads=(), writes=(f"wstr{k % 2}",))
                    dma("q", s_d[k * P:(k + 1) * P, half * D:(half + 1) * D],
                        wb[:], reads=(f"wstr{k % 2}",), writes=(f"smw_{tag}",))

        qrot = [0]

        def quant_gen(tag, w_d, rows, cols, mc_col, sw_col, sT_d, mi):
            ngr = rows * cols // (P * 4096)
            per = 4096 // cols
            ncb = cols // P
            ldb = [zbuf2[0][:, 0:4096], zbuf2[1][:, 0:4096]]
            ldn = ["zbuf2_0", "zbuf2_1"]
            s1b = [upf[0][:, 0:2048], upf[0][:, 2048:4096]]
            s1n = ["upf0A", "upf0B"]
            s2b = [upf[1][:, 0:2048], upf[1][:, 2048:4096]]
            s2n = ["upf1A", "upf1B"]
            q3b = [rscr[:, 0:2048], rscr[:, 0:2048]]
            q3n = ["rscrA", "rscrA"]

            def load(g_):
                lb, ln_ = ldb[g_ % 2], ldn[g_ % 2]
                src_ = w_d[g_ * per * P:(g_ + 1) * per * P, :]
                if per > 1:
                    src_ = src_.rearrange("(a b) cc -> b a cc", b=P)
                    dst_ = lb.rearrange("z (a cc) -> z a cc", a=per)
                else:
                    dst_ = lb
                dma("r", dst_, src_, reads=(), writes=(ln_,))
                return lb, ln_

            for g_ in range(ngr):
                lb, ln_ = load(g_)
                for hf in range(2):
                    OP("s", lambda h, lb=lb, g_=g_, hf=hf, mi=mi:
                       h.activation(
                           out=s1b[hf], in_=lb[:, hf * 2048:(hf + 1) * 2048],
                           func=AF.Abs,
                           accum_out=accq[:, (mi % 2) * 16 + 2 * g_ + hf:
                                          (mi % 2) * 16 + 2 * g_ + hf + 1]),
                       reads=(ln_,), writes=(s1n[hf], "accq"))
                if g_ % 4 == 3:
                    yield
            OP("t", lambda h, ngr=ngr, mi=mi: h.matmul(
                pb[0][0:1, 0:2 * ngr], ones[:, 0:1],
                accq[:, (mi % 2) * 16:(mi % 2) * 16 + 2 * ngr],
                start=True, stop=True),
               reads=("accq", "ones"), writes=("pb0",))
            OP("v", lambda h, ngr=ngr, mi=mi: h.tensor_reduce(
                out=qsc[0:1, mi:mi + 1], in_=pb[0][0:1, 0:2 * ngr], axis=AX.X,
                op=ALU.add), reads=("pb0",), writes=("qsc",))
            OP("v", lambda h, mi=mi: h.tensor_scalar(
                out=qsc[0:1, mi:mi + 1], in0=qsc[0:1, mi:mi + 1],
                scalar1=1.0 / (rows * cols), scalar2=1e-5, op0=ALU.mult,
                op1=ALU.max), reads=("qsc",), writes=("qsc",))
            dma("q", ssc_d[0:1, mi:mi + 1], qsc[0:1, mi:mi + 1],
                reads=("qsc",), writes=(f"ssc{mi}",))
            dma("q", mc_col[:], ssc_d[0:1, mi:mi + 1].broadcast_to((P, 1)),
                reads=(f"ssc{mi}",), writes=(f"mc_{tag}",))
            OP("v", lambda h: h.reciprocal(out=sw_col[:], in_=mc_col[:]),
               reads=(f"mc_{tag}",), writes=(f"sw_{tag}",))
            yield
            for g_ in range(ngr):
                lb, ln_ = load(g_)
                for hf in range(2):
                    hsl = slice(hf * 2048, (hf + 1) * 2048)
                    OP("s", lambda h, lb=lb, hsl=hsl, hf=hf: h.activation(
                        out=s1b[hf], in_=lb[:, hsl], func=AF.Sign,
                        scale=sw_col[:, 0:1],
                        bias=consts[:, C_NHALF:C_NHALF + 1]),
                       reads=(ln_, f"sw_{tag}", "consts"), writes=(s1n[hf],))
                    OP("s", lambda h, lb=lb, hsl=hsl, hf=hf: h.activation(
                        out=s2b[hf], in_=lb[:, hsl], func=AF.Sign,
                        scale=sw_col[:, 0:1],
                        bias=consts[:, C_PHALF:C_PHALF + 1]),
                       reads=(ln_, f"sw_{tag}", "consts"), writes=(s2n[hf],))
                    OP("v", lambda h, hf=hf: h.tensor_tensor(
                        out=q3b[hf], in0=s1b[hf], in1=s2b[hf], op=ALU.add),
                       reads=(s1n[hf], s2n[hf]), writes=(q3n[hf],))
                    for cb4 in range(4):
                        bank = pb[6 + cb4 % 2]
                        bknm = f"pb{6 + cb4 % 2}"
                        for j in range(4):
                            blk = hf * 16 + cb4 * 4 + j
                            a, cb = divmod(blk, ncb)
                            off = (a * cols + cb * P) - hf * 2048
                            OP("t", lambda h, off=off, j=j, bank=bank, hf=hf:
                               h.transpose(bank[:, j * P:(j + 1) * P],
                                           q3b[hf][:, off:off + P],
                                           ident[:]),
                               reads=(q3n[hf], "ident"), writes=(bknm,))
                        blk0 = hf * 16 + cb4 * 4
                        a0, cb0 = divmod(blk0, ncb)
                        ti0 = g_ * per + a0
                        r = qrot[0] % 4
                        qrot[0] += 1
                        OP("v", lambda h, bank=bank, r=r: h.tensor_copy(
                            out=wsg[0][:, r * 512:(r + 1) * 512],
                            in_=bank[:, 0:512]),
                           reads=(bknm,), writes=(f"wsg0_{r}",))
                        dma("q", sT_d[cb0 * P:cb0 * P + 512,
                                      ti0 * P:(ti0 + 1) * P]
                            .rearrange("(a b) cc -> b a cc", b=P),
                            wsg[0][:, r * 512:(r + 1) * 512]
                            .rearrange("z (a cc) -> z a cc", a=4),
                            reads=(f"wsg0_{r}",), writes=(f"sT_{tag}",))
                yield

        qgens = [quant_gen("wg", wg_d, DI, D, mcg, swg_c, swgT_d, 0),
                 quant_gen("wu", wu_d, DI, D, mcu, swu_c, swuT_d, 1),
                 quant_gen("wd", wd_d, D, DI, mcd, swd_c, swdT_d, 2)]

        qfence = [False]

        def qstep(k):
            while k > 0 and qgens:
                try:
                    next(qgens[0])
                    k -= 1
                except StopIteration:
                    qgens.pop(0)
            if not qgens and not qfence[0]:
                qfence[0] = True
                OP("v", lambda h: h.engine_nop(), reads=(),
                   writes=tuple(f"xcv{i}" for i in range(NDB)) + tuple(
                       f"zbq{m2}_{q}" for m2 in range(2)
                       for q in range(4)) + (
                       "upf0", "upf1", "rscr", "zbuf2_0", "zbuf2_1", "zqT",
                       "upf0A", "upf0B", "upf1A", "upf1B", "rscrA", "zqTA",
                       "wsg0", "wsg0_0", "wsg0_1", "wsg0_2", "wsg0_3"))

        def layernorm(bufs, names, out_bufs, out_names, st, stn, scr, scrn):
            nb = len(bufs)
            scrns = scrn if isinstance(scrn, tuple) else (scrn,)
            (ssum, sssq, smean, svar, srstd, snmr, stmp) = st
            for i, (b, nm) in enumerate(zip(bufs, names)):
                OP("s", lambda h, b=b, i=i: h.activation(
                    out=scr, in_=b, func=AF.Square,
                    accum_out=sssq[:, i:i + 1]),
                   reads=(nm,), writes=scrns + (stn,))
                OP("s", lambda h, b=b, i=i: h.activation(
                    out=scr, in_=b, func=AF.Identity,
                    accum_out=ssum[:, i:i + 1]),
                   reads=(nm,), writes=scrns + (stn,))
            OP("v", lambda h, nb=nb: h.tensor_scalar(
                out=smean[:, 0:nb], in0=ssum[:, 0:nb], scalar1=1.0 / D,
                scalar2=None, op0=ALU.mult), reads=(stn,), writes=(stn,))
            OP("v", lambda h, nb=nb: h.tensor_scalar(
                out=svar[:, 0:nb], in0=sssq[:, 0:nb], scalar1=1.0 / D,
                scalar2=None, op0=ALU.mult), reads=(stn,), writes=(stn,))
            OP("v", lambda h, nb=nb: h.tensor_tensor(
                out=stmp[:, 0:nb], in0=smean[:, 0:nb], in1=smean[:, 0:nb],
                op=ALU.mult), reads=(stn,), writes=(stn,))
            OP("v", lambda h, nb=nb: h.tensor_tensor(
                out=svar[:, 0:nb], in0=svar[:, 0:nb], in1=stmp[:, 0:nb],
                op=ALU.subtract), reads=(stn,), writes=(stn,))
            OP("s", lambda h, nb=nb: h.activation(
                out=srstd[:, 0:nb], in_=svar[:, 0:nb], func=AF.Ln,
                bias=consts[:, C_EPS:C_EPS + 1]),
               reads=(stn, "consts"), writes=(stn,))
            OP("s", lambda h, nb=nb: h.activation(
                out=srstd[:, 0:nb], in_=srstd[:, 0:nb], func=AF.Exp,
                scale=consts[:, C_NHALFSC:C_NHALFSC + 1]),
               reads=(stn, "consts"), writes=(stn,))
            OP("v", lambda h, nb=nb: h.scalar_tensor_tensor(
                out=snmr[:, 0:nb], in0=smean[:, 0:nb], scalar=-1.0,
                in1=srstd[:, 0:nb], op0=ALU.mult, op1=ALU.mult),
               reads=(stn,), writes=(stn,))
            for i, (b, ob, onm) in enumerate(zip(bufs, out_bufs, out_names)):
                OP("s", lambda h, b=b, ob=ob, i=i: h.activation(
                    out=ob, in_=b, func=AF.Identity,
                    scale=srstd[:, i:i + 1], bias=snmr[:, i:i + 1]),
                   reads=(names[i], stn), writes=(onm,))

        ST1 = (st_sum, st_ssq, st_mean, st_var, st_rstd, st_nmr, st_tmp)
        ST2 = (s2_sum, s2_ssq, s2_mean, s2_var, s2_rstd, s2_nmr, s2_tmp)

        def round_chain(t_ap, t_nm, ci_ap, ci_nm, c1_ap, c1_nm, dp_ap, dp_nm,
                        out_ap, out_nm, rng=1, collect=False):
            ops = []
            OP = (lambda e, f, reads, writes:
                  ops.append((e, f, reads, writes))) if collect else prog.op
            OP("s", lambda h: h.activation(out=ci_ap, in_=t_ap, func=AF.Copy),
               reads=(t_nm,), writes=(ci_nm,))
            OP("v", lambda h: h.tensor_scalar(
                out=c1_ap, in0=ci_ap, scalar1=-float(rng), scalar2=None,
                op0=ALU.add),
               reads=(ci_nm,), writes=(c1_nm,))
            OP("v", lambda h: h.tensor_tensor(
                out=dp_ap, in0=t_ap, in1=c1_ap, op=ALU.subtract),
               reads=(t_nm, c1_nm), writes=(dp_nm,))
            for j in range(2 * rng):
                alu = ALU.is_ge if j == rng - 1 else ALU.is_gt
                dst_ap = out_ap if j == 2 * rng - 1 else c1_ap
                dst_nm = out_nm if j == 2 * rng - 1 else c1_nm
                OP("v", lambda h, j=j, alu=alu, dst_ap=dst_ap:
                   h.scalar_tensor_tensor(
                       out=dst_ap, in0=dp_ap, scalar=j + 0.5, in1=c1_ap,
                       op0=alu, op1=ALU.add),
                   reads=(dp_nm, c1_nm), writes=(dst_nm,))
            return ops

        def emit_front_p1(c):
            xb = xbuf[c % 3]
            xbn = [f"xbuf{c % 3}_{i}" for i in range(2)]
            for m in range(2):
                dma("q", xb[m][:], x_d[c * LC + m * P:c * LC + (m + 1) * P, :],
                    reads=(), writes=(xbn[m],))
            layernorm([xb[m][:] for m in range(2)], xbn,
                      [xzn[m][:] for m in range(2)],
                      [f"xzn{m}" for m in range(2)], ST1, "st1",
                      sgt[:, 0:D], ("sgt0", "sgt1", "sgt2", "sgt3"))
            prog.drain(2)
            for k in range(NDB):
                bank, bknm = pb[6 + k % 2], f"pb{6 + k % 2}"
                for m in range(2):
                    OP("t", lambda h, k=k, m=m, bank=bank: h.transpose(
                        bank[:, m * P:(m + 1) * P],
                        xzn[m][:, k * P:(k + 1) * P], ident[:]),
                       reads=(f"xzn{m}", "ident"), writes=(bknm,))
                OP("s", lambda h, k=k, bank=bank: h.activation(
                    out=xnT[:, k * LC:(k + 1) * LC], in_=bank[:, 0:LC],
                    func=AF.Copy), reads=(bknm,), writes=(f"xnT{k}",))
                if k % 2 == 1:
                    prog.drain(1)
            for db in range(NDB):
                o = db * (LC + 4)
                if c == 0:
                    OP("v", lambda h, o=o: h.memset(xpad[:, o:o + 3], 0.0),
                       reads=(f"xpad{db}",), writes=(f"xpad{db}",))
                else:
                    OP("v", lambda h, o=o: h.tensor_copy(
                        out=xpad[:, o:o + 3], in_=xpad[:, o + LC:o + LC + 3]),
                       reads=(f"xpad{db}",), writes=(f"xpad{db}",))
            for half in range(2):
                for k in range(NDB):
                    wb = wstr[k % 2]
                    dma("q", wb[:], sip_d[k * P:(k + 1) * P,
                                          half * D:(half + 1) * D],
                        reads=("smw_ip",), writes=(f"wstr{k % 2}",))
                    for m in range(8):
                        OP("t", lambda h, wb=wb, k=k, m=m: h.matmul(
                            pb[m][:, 0:LC], wb[:, m * P:(m + 1) * P],
                            xnT[:, k * LC:(k + 1) * LC],
                            start=(k == 0), stop=(k == NDB - 1)),
                           reads=(f"wstr{k % 2}", f"xnT{k}"),
                           writes=(f"pb{m}",))
                for m in range(8):
                    mt = half * 8 + m
                    if half == 0:
                        o = m * (LC + 4)
                        OP("s", lambda h, m=m, mt=mt, o=o: h.activation(
                            out=xpad[:, o + 3:o + 3 + LC], in_=pb[m][:, 0:LC],
                            func=AF.Identity, bias=ipb[:, mt:mt + 1]),
                           reads=(f"pb{m}", "ipb", f"xpad{m}"),
                           writes=(f"xpad{m}",))
                    else:
                        OP("s", lambda h, m=m, mt=mt: h.activation(
                            out=sgt[:, m * LC:(m + 1) * LC],
                            in_=pb[m][:, 0:LC],
                            func=AF.Silu, bias=ipb[:, mt:mt + 1]),
                           reads=(f"pb{m}", "ipb"), writes=(f"sgt{m}",))
                    if m % 2 == 1:
                        prog.drain(2)
        def emit_front_p2(c):
            for db in range(NDB):
                o = db * (LC + 4)
                OP("v", lambda h, o=o, db=db: h.tensor_scalar(
                    out=pool6[0][:], in0=xpad[:, o:o + LC],
                    scalar1=convw[:, db * 4:db * 4 + 1], scalar2=None,
                    op0=ALU.mult),
                   reads=(f"xpad{db}", "convw"), writes=("p6_0",))
                for k in range(1, 4):
                    OP("v", lambda h, o=o, db=db, k=k: h.scalar_tensor_tensor(
                        out=pool6[0][:], in0=xpad[:, o + k:o + k + LC],
                        scalar=convw[:, db * 4 + k:db * 4 + k + 1],
                        in1=pool6[0][:], op0=ALU.mult, op1=ALU.add),
                       reads=(f"xpad{db}", "convw", "p6_0"), writes=("p6_0",))
                OP("s", lambda h, db=db: h.activation(
                    out=xcv[:, db * LC:(db + 1) * LC], in_=pool6[0][:],
                    func=AF.Silu, bias=convb[:, db:db + 1]),
                   reads=("p6_0", "convb"), writes=(f"xcv{db}",))
                OP("s", lambda h, db=db: h.activation(
                    out=xnT[:, db * LC:(db + 1) * LC], in_=pool6[0][:],
                    func=AF.Silu, bias=convb[:, db:db + 1]),
                   reads=("p6_0", "convb"), writes=(f"xnT{db}",))
                prog.drain(1)
            for k in range(NDB):
                OP("t", lambda h, k=k: h.matmul(
                    pb[6][0:32, 0:LC], bcwT[:, k * 32:(k + 1) * 32],
                    xnT[:, k * LC:(k + 1) * LC],
                    start=(k == 0), stop=(k == NDB - 1)),
                   reads=("bcwT", f"xnT{k}"), writes=("pb6",))
            OP("s", lambda h: h.activation(out=bct[:], in_=pb[6][0:32, 0:LC],
                                           func=AF.Identity, bias=bcb[:, 0:1]),
               reads=("pb6", "bcb"), writes=("bct",))
            dma("q", sbc_d[:, :], bct[:], reads=("bct",), writes=("sbc",))
            for k in range(NDB):
                wb = wstr[k % 2]
                dma("q", wb[:], sdt_d[k * P:(k + 1) * P, :],
                    reads=("smw_dt",), writes=(f"wstr{k % 2}",))
                for m in range(8):
                    OP("t", lambda h, wb=wb, k=k, m=m: h.matmul(
                        pb[m][:, 0:LC], wb[:, m * P:(m + 1) * P],
                        xnT[:, k * LC:(k + 1) * LC],
                        start=(k == 0), stop=(k == NDB - 1)),
                       reads=(f"wstr{k % 2}", f"xnT{k}"), writes=(f"pb{m}",))
                prog.drain(2)
            for m in range(8):
                OP("s", lambda h, m=m: h.activation(
                    out=pool6[1][:], in_=pb[m][:, 0:LC], func=AF.Abs,
                    bias=dtb[:, m:m + 1]),
                   reads=(f"pb{m}", "dtb"), writes=("p6_1",))
                OP("s", lambda h: h.activation(
                    out=pool6[1][:], in_=pool6[1][:], func=AF.Exp,
                    scale=-1.0),
                   reads=("p6_1",), writes=("p6_1",))
                OP("s", lambda h: h.activation(
                    out=pool6[1][:], in_=pool6[1][:], func=AF.Ln, bias=1.0),
                   reads=("p6_1",), writes=("p6_1",))
                OP("s", lambda h, m=m: h.activation(
                    out=pool6[2][:], in_=pb[m][:, 0:LC], func=AF.Relu,
                    bias=dtb[:, m:m + 1]),
                   reads=(f"pb{m}", "dtb"), writes=("p6_2",))
                OP("v", lambda h, m=m: h.tensor_tensor(
                    out=dtt[:, m * LC:(m + 1) * LC], in0=pool6[1][:],
                    in1=pool6[2][:], op=ALU.add),
                   reads=("p6_1", "p6_2"), writes=(f"dtt{m}",))
            for db in range(NDB):
                o = db * (LC + 4)
                OP("v", lambda h, db=db, o=o: h.tensor_tensor(
                    out=xpad[:, o:o + LC], in0=dtt[:, db * LC:(db + 1) * LC],
                    in1=xcv[:, db * LC:(db + 1) * LC], op=ALU.mult),
                   reads=(f"dtt{db}", f"xcv{db}", f"xpad{db}"),
                   writes=(f"xpad{db}",))
            for db in range(NDB):
                OP("s", lambda h, db=db: h.activation(
                    out=xcv[:, db * LC:(db + 1) * LC],
                    in_=xcv[:, db * LC:(db + 1) * LC], func=AF.Copy,
                    scale=dcol[:, db:db + 1]),
                   reads=(f"xcv{db}", "dcol"), writes=(f"xcv{db}",))

        pend_carry = []

        def push_carry(hh, hhnm, col):
            pend_carry.append((hh, hhnm, col))

        def flush_carry(k=None):
            take = len(pend_carry) if k is None else k
            for _ in range(take):
                hh, hhnm, col = pend_carry.pop(0)
                OP("s", lambda h, hh=hh, col=col: h.activation(
                    out=carry[:, col:col + 1], in_=hh[:, LC - 1:LC],
                    func=AF.Copy),
                   reads=(hhnm,), writes=("carry",))

        def scan_group(c, n, g, bB, bBn, bC, bCn, tail=False):
            flush_carry()
            dbs = range(g * 4, g * 4 + 4)

            def bufs(db):
                i = n * NDB + db
                da, danm = pool6[i % 4], f"p6_{i % 4}"
                bb, bbnm = pool6[4 + i % 4], f"p6_{4 + i % 4}"
                hx = 8 + i % 5
                hh, hhnm = pool6[hx], f"p6_{hx}"
                return da, danm, bb, bbnm, hh, hhnm, db * NST + n, db * (LC + 4)

            for db in dbs:
                da, danm, bb, bbnm, hh, hhnm, col, o = bufs(db)
                OP("s", lambda h, da=da, db=db, col=col: h.activation(
                    out=da[:], in_=dtt[:, db * LC:(db + 1) * LC],
                    func=AF.Exp, scale=a_t[:, col:col + 1]),
                   reads=(f"dtt{db}", "a_t"), writes=(danm,))
            for db in dbs:
                da, danm, bb, bbnm, hh, hhnm, col, o = bufs(db)
                OP("v", lambda h, bb=bb, o=o, bB=bB: h.tensor_tensor(
                    out=bb[:], in0=xpad[:, o:o + LC], in1=bB[:],
                    op=ALU.mult),
                   reads=(f"xpad{db}", bBn), writes=(bbnm,))
            for db in dbs:
                da, danm, bb, bbnm, hh, hhnm, col, o = bufs(db)
                OP("v", lambda h, hh=hh, da=da, bb=bb, col=col:
                   h.tensor_tensor_scan(
                       out=hh[:], data0=da[:], data1=bb[:],
                       initial=carry[:, col:col + 1],
                       op0=ALU.mult, op1=ALU.add),
                   reads=(danm, bbnm, "carry"), writes=(hhnm,))
            for db in dbs:
                da, danm, bb, bbnm, hh, hhnm, col, o = bufs(db)
                push_carry(hh, hhnm, col)
            for db in dbs:
                da, danm, bb, bbnm, hh, hhnm, col, o = bufs(db)
                OP("v", lambda h, hh=hh, bb=bb, bC=bC: h.tensor_tensor(
                    out=bb[:], in0=hh[:], in1=bC[:], op=ALU.mult),
                   reads=(hhnm, bCn), writes=(bbnm,))
            for db in dbs:
                da, danm, bb, bbnm, hh, hhnm, col, o = bufs(db)
                OP("v", lambda h, bb=bb, db=db: h.tensor_tensor(
                    out=xcv[:, db * LC:(db + 1) * LC], in0=bb[:],
                    in1=xcv[:, db * LC:(db + 1) * LC], op=ALU.add),
                   reads=(bbnm, f"xcv{db}"), writes=(f"xcv{db}",))
            if tail:
                for db in dbs:
                    OP("v", lambda h, db=db: h.tensor_tensor(
                        out=xnT[:, db * LC:(db + 1) * LC],
                        in0=xcv[:, db * LC:(db + 1) * LC],
                        in1=sgt[:, db * LC:(db + 1) * LC], op=ALU.mult),
                       reads=(f"xcv{db}", f"sgt{db}"), writes=(f"xnT{db}",))
                for db in dbs:
                    k = db
                    wb = wstr[k % 2]
                    dma("q", wb[:], sout_d[k * P:(k + 1) * P, :],
                        reads=("smw_out",), writes=(f"wstr{k % 2}",))
                    for m in range(2):
                        for nb in range(2):
                            OP("t", lambda h, wb=wb, k=k, m=m, nb=nb: h.matmul(
                                pb[m * 2 + nb][:],
                                xnT[:, k * LC + m * P:k * LC + (m + 1) * P],
                                wb[:, nb * 512:(nb + 1) * 512],
                                start=(k == 0), stop=(k == NDB - 1)),
                               reads=(f"wstr{k % 2}", f"xnT{k}"),
                               writes=(f"pb{m * 2 + nb}",))

        def scan_dmas(n):
            bB, bBn = bcc[n % 2], f"bcc{n % 2}"
            bC, bCn = bcc[2 + n % 2], f"bcc{2 + n % 2}"
            dma("g", bB[:], sbc_d[n:n + 1, :].broadcast_to((P, LC)),
                reads=("sbc",), writes=(bBn,))
            dma("g", bC[:], sbc_d[NST + n:NST + n + 1, :]
                .broadcast_to((P, LC)),
                reads=("sbc",), writes=(bCn,))
            return bB, bBn, bC, bCn

        def emit_scan(c, n0, n1):
            for n in range(n0, n1):
                bB, bBn, bC, bCn = scan_dmas(n)
                scan_group(c, n, 0, bB, bBn, bC, bCn)
                scan_group(c, n, 1, bB, bBn, bC, bCn)

        def emit_tail(c):
            n = NST - 1
            bB, bBn, bC, bCn = scan_dmas(n)
            scan_group(c, n, 0, bB, bBn, bC, bCn, tail=True)
            scan_group(c, n, 1, bB, bBn, bC, bCn, tail=True)
            flush_carry()

        def emit_tail2(c):
            xb = xbuf[c % 3]
            xbn = [f"xbuf{c % 3}_{i}" for i in range(2)]
            for m in range(2):
                for nb in range(2):
                    OP("v", lambda h, m=m, nb=nb: h.scalar_tensor_tensor(
                        out=xzn[m][:, nb * 512:(nb + 1) * 512],
                        in0=pb[m * 2 + nb][:], scalar=0.0,
                        in1=xzn[m][:, nb * 512:(nb + 1) * 512],
                        op0=ALU.add, op1=ALU.add),
                       reads=(f"pb{m * 2 + nb}", f"xzn{m}"),
                       writes=(f"xzn{m}",))
                OP("v", lambda h, m=m: h.tensor_tensor(
                    out=xzn[m][:], in0=xzn[m][:], in1=outb[:], op=ALU.add),
                   reads=(f"xzn{m}", "outb"), writes=(f"xzn{m}",))
            prog.drain(3)
            layernorm([xzn[m][:] for m in range(2)],
                      [f"xzn{m}" for m in range(2)],
                      [xzn[m][:] for m in range(2)],
                      [f"xzn{m}" for m in range(2)], ST1, "st1",
                      sgt[:, 0:D], ("sgt0", "sgt1", "sgt2", "sgt3"))
            prog.drain(1)
            for m in range(2):
                OP("v", lambda h, m=m: h.tensor_tensor(
                    out=xzn[m][:], in0=xzn[m][:], in1=lng[:], op=ALU.mult),
                   reads=(f"xzn{m}", "lng"), writes=(f"xzn{m}",))
                OP("v", lambda h, m=m: h.tensor_tensor(
                    out=xzn[m][:], in0=xzn[m][:], in1=lnb[:], op=ALU.add),
                   reads=(f"xzn{m}", "lnb"), writes=(f"xzn{m}",))
            prog.drain(2)
            layernorm([xzn[m][:] for m in range(2)],
                      [f"xzn{m}" for m in range(2)],
                      [xzn[m][:] for m in range(2)],
                      [f"xzn{m}" for m in range(2)], ST1, "st1",
                      sgt[:, 0:D], ("sgt0", "sgt1", "sgt2", "sgt3"))
            prog.drain(1)
            for m in range(2):
                OP("v", lambda h, m=m, xb=xb: h.tensor_tensor(
                    out=xb[m][:], in0=xb[m][:], in1=xzn[m][:], op=ALU.add),
                   reads=(xbn[m], f"xzn{m}"), writes=(xbn[m],))
            prog.drain(2)
            layernorm([xb[m][:] for m in range(2)], xbn,
                      [xzn[m][:] for m in range(2)],
                      [f"xzn{m}" for m in range(2)], ST1, "st1",
                      sgt[:, 0:D], ("sgt0", "sgt1", "sgt2", "sgt3"))
            prog.drain(1)
            for m in range(2):
                OP("v", lambda h, m=m: h.tensor_reduce(
                    out=fx_mx[:, m:m + 1], in_=xzn[m][:], axis=AX.X,
                    op=ALU.max, apply_absolute_value=True),
                   reads=(f"xzn{m}",), writes=("fx_mx",))
                OP("v", lambda h, m=m: h.tensor_scalar(
                    out=fx_mx[:, m:m + 1], in0=fx_mx[:, m:m + 1],
                    scalar1=1e-5, scalar2=None, op0=ALU.max),
                   reads=("fx_mx",), writes=("fx_mx",))
                OP("v", lambda h, m=m: h.tensor_scalar(
                    out=fx_sxi[:, m:m + 1], in0=fx_mx[:, m:m + 1],
                    scalar1=1.0 / 127.0, scalar2=None, op0=ALU.mult),
                   reads=("fx_mx",), writes=("fx_sxi",))
                OP("v", lambda h, m=m: h.reciprocal(
                    out=fx_sx[:, m:m + 1], in_=fx_sxi[:, m:m + 1]),
                   reads=("fx_sxi",), writes=("fx_sx",))
                OP("v", lambda h, m=m: h.scalar_tensor_tensor(
                    out=sc_g[:, m:m + 1], in0=fx_sxi[:, m:m + 1], scalar=0.5,
                    in1=mcg[:, 0:1], op0=ALU.mult, op1=ALU.mult),
                   reads=("fx_sxi", "mc_wg"), writes=("sc_g",))
                OP("v", lambda h, m=m: h.scalar_tensor_tensor(
                    out=sc_u[:, m:m + 1], in0=fx_sxi[:, m:m + 1], scalar=0.5,
                    in1=mcu[:, 0:1], op0=ALU.mult, op1=ALU.mult),
                   reads=("fx_sxi", "mc_wu"), writes=("sc_u",))
                for qq in range(D // LC):
                    OP("s", lambda h, m=m, qq=qq: h.activation(
                        out=pool6[0][:], in_=xzn[m][:, qq * LC:(qq + 1) * LC],
                        func=AF.Copy, scale=fx_sx[:, m:m + 1]),
                       reads=(f"xzn{m}", "fx_sx"), writes=("p6_0",))
                    round_chain(pool6[0][:], "p6_0", ffci2[:], "ffci2",
                                pool6[1][:], "p6_1", pool6[2][:], "p6_2",
                                pool6[3][:], "p6_3", rng=2)
                    bank, bknm = pb[6 + qq % 2], f"pb{6 + qq % 2}"
                    for j in range(LC // P):
                        OP("t", lambda h, j=j, bank=bank: h.transpose(
                            bank[:, j * P:(j + 1) * P],
                            pool6[3][:, j * P:(j + 1) * P], ident[:]),
                           reads=("p6_3", "ident"), writes=(bknm,))
                    OP("s", lambda h, m=m, qq=qq, bank=bank: h.activation(
                        out=xqT[:, :].rearrange("z (k il) -> z k il", il=256)
                        [:, qq * 2:qq * 2 + 2, m * P:(m + 1) * P],
                        in_=bank[:, 0:LC].rearrange(
                            "z (k il) -> z k il", il=P),
                        func=AF.Copy),
                       reads=(bknm,), writes=("xqT",))
                    prog.drain(1)

        FFNA_PC = ((swgT_d, "sc_g", sc_g,
                    [zbuf2[0], zbuf2[1]], ["zbuf2_0", "zbuf2_1"],
                    AF.Silu, "wg"),
                   (swuT_d, "sc_u", sc_u,
                    [upf[0], upf[1]], ["upf0", "upf1"],
                    AF.Copy, "wu"))

        def emit_ffnA_mms(c, pc):
            sT_d, scnm, scol, zdst, zdnm, actf, wtag = FFNA_PC[pc // 2]
            half = pc % 2
            for k in range(NDB):
                wb = wsg[k % 2]
                dma("q", wb[:], sT_d[k * P:(k + 1) * P,
                                     half * 2048:(half + 1) * 2048],
                    reads=(f"sT_{wtag}",), writes=(f"wsg{k % 2}",))
                for m2 in range(2):
                    for nb in range(4):
                        OP("t", lambda h, wb=wb, k=k, m2=m2,
                           nb=nb: h.matmul(
                               pb[m2 * 4 + nb][:],
                               xqT[:, k * 256 + m2 * P:
                                   k * 256 + (m2 + 1) * P],
                               wb[:, nb * 512:(nb + 1) * 512],
                               start=(k == 0), stop=(k == NDB - 1)),
                           reads=(f"wsg{k % 2}", "xqT"),
                           writes=(f"pb{m2 * 4 + nb}",))

        def emit_ffnA_evacs(c, pc):
            sT_d, scnm, scol, zdst, zdnm, actf, wtag = FFNA_PC[pc // 2]
            half = pc % 2
            for m2 in range(2):
                for nb in range(4):
                    off = half * 2048 + nb * 512
                    wset = (zdnm[m2],)
                    if wtag == "wg":
                        wset += (f"zbq{m2}_{off // 1024}",)
                    OP("s", lambda h, m2=m2, nb=nb, off=off,
                       zdst=zdst, scol=scol, actf=actf:
                       h.activation(
                           out=zdst[m2][:, off:off + 512],
                           in_=pb[m2 * 4 + nb][:], func=actf,
                           scale=scol[:, m2:m2 + 1]),
                       reads=(f"pb{m2 * 4 + nb}", scnm),
                       writes=wset)

        def emit_ffnA(c):
            for pc in range(4):
                emit_ffnA_mms(c, pc)
                emit_ffnA_evacs(c, pc)

        def emit_ffnB(c):
            zq_nm = [tuple(f"zbq{m2}_{q}" for q in range(4)) for m2 in range(2)]
            for m2 in range(2):
                OP("v", lambda h, m2=m2: h.tensor_tensor(
                    out=zbuf2[m2][:], in0=zbuf2[m2][:], in1=upf[m2][:],
                    op=ALU.mult),
                   reads=(f"zbuf2_{m2}",) + zq_nm[m2] + (f"upf{m2}",),
                   writes=(f"zbuf2_{m2}",) + zq_nm[m2])
                OP("v", lambda h, m2=m2: h.tensor_reduce(
                    out=fz_mx[:, m2:m2 + 1], in_=zbuf2[m2][:], axis=AX.X,
                    op=ALU.max, apply_absolute_value=True),
                   reads=(f"zbuf2_{m2}",) + zq_nm[m2], writes=("fz_mx",))
            OP("v", lambda h: h.tensor_scalar(
                out=fz_mx[:, 0:2], in0=fz_mx[:, 0:2], scalar1=1e-5,
                scalar2=None, op0=ALU.max), reads=("fz_mx",),
               writes=("fz_mx",))
            OP("v", lambda h: h.tensor_scalar(
                out=fz_szi[:, 0:2], in0=fz_mx[:, 0:2], scalar1=1.0 / 127.0,
                scalar2=None, op0=ALU.mult), reads=("fz_mx",),
               writes=("fz_szi",))
            OP("v", lambda h: h.reciprocal(
                out=fz_sz[:, 0:2], in_=fz_szi[:, 0:2]),
               reads=("fz_szi",), writes=("fz_sz",))
            for m2 in range(2):
                OP("v", lambda h, m2=m2: h.scalar_tensor_tensor(
                    out=sc_d[:, m2:m2 + 1], in0=fz_szi[:, m2:m2 + 1],
                    scalar=0.5, in1=mcd[:, 0:1], op0=ALU.mult, op1=ALU.mult),
                   reads=("fz_szi", "mc_wd"), writes=("sc_d",))
                OP("s", lambda h, m2=m2: h.activation(
                    out=zbuf2[m2][:], in_=zbuf2[m2][:], func=AF.Copy,
                    scale=fz_sz[:, m2:m2 + 1]),
                   reads=(f"zbuf2_{m2}", "fz_sz") + zq_nm[m2],
                   writes=(f"zbuf2_{m2}",) + zq_nm[m2])
                for hp in range(2):
                    pair = []
                    for hf in (hp * 2, hp * 2 + 1):
                        sl = slice(hf * 1024, (hf + 1) * 1024)
                        scrb = rscr if hf % 2 == 0 else upf[1]
                        scrn = "rscr" if hf % 2 == 0 else "upf1"
                        fsl = slice((hf % 2) * 1024, (hf % 2) * 1024 + 1024)
                        pair.append(round_chain(
                            zbuf2[m2][:, sl], f"zbq{m2}_{hf}",
                            ffci[:, fsl], f"ffci{hf % 2}",
                            scrb[:, 0:1024], scrn,
                            scrb[:, 1024:2048], scrn,
                            zbuf2[m2][:, sl], f"zbq{m2}_{hf}", rng=4,
                            collect=True))
                    for a, b in zip(*pair):
                        OP(*a)
                        OP(*b)

        def emit_ffnC_kh(c, kh):
            if True:
                for k4l in range(4):
                    k4 = kh * 4 + k4l
                    for m2 in range(2):
                        bank = pb[6 + (k4l * 2 + m2) % 2]
                        bknm = f"pb{6 + (k4l * 2 + m2) % 2}"
                        for j in range(4):
                            k = k4 * 4 + j
                            OP("t", lambda h, k=k, j=j, bank=bank,
                               m2=m2: h.transpose(
                                   bank[:, j * P:(j + 1) * P],
                                   zbuf2[m2][:, k * P:(k + 1) * P],
                                   ident[:]),
                               reads=(f"zbq{m2}_{k // 8}", "ident"),
                               writes=(bknm,))
                        OP("s", lambda h, k4l=k4l, m2=m2, bank=bank:
                           h.activation(
                               out=zqT[:, k4l * 1024 + m2 * 512:
                                       k4l * 1024 + (m2 + 1) * 512],
                               in_=bank[:, 0:512], func=AF.Copy),
                           reads=(bknm,), writes=("zqT",))
                for k2l in range(8):
                    k2 = kh * 8 + k2l
                    wb = wsd[k2 % 2]
                    dma("q", wb[:].rearrange("z (a cc) -> z a cc", a=2),
                        swdT_d[k2 * 256:(k2 + 1) * 256, :]
                        .rearrange("(a b) cc -> b a cc", b=P),
                        reads=("sT_wd",), writes=(f"wsd{k2 % 2}",))
                    for j in range(2):
                        k = k2 * 2 + j
                        kl = k - kh * 16
                        k4l, jj = kl // 4, kl % 4
                        for m2 in range(2):
                            col = k4l * 1024 + m2 * 512 + jj * P
                            for nb in range(2):
                                OP("t", lambda h, wb=wb, k=k, j=j, m2=m2,
                                   nb=nb, col=col: h.matmul(
                                       pb[m2 * 2 + nb][:],
                                       zqT[:, col:col + P],
                                       wb[:, j * 1024 + nb * 512:
                                          j * 1024 + (nb + 1) * 512],
                                       start=(k == 0), stop=(k == 31)),
                                   reads=(f"wsd{k2 % 2}", "zqT"),
                                   writes=(f"pb{m2 * 2 + nb}",))
        def emit_ffnC_fin(c):
            xb = xbuf[c % 3]
            xbn = [f"xbuf{c % 3}_{i}" for i in range(2)]
            for m2 in range(2):
                for nb in range(2):
                    OP("s", lambda h, m2=m2, nb=nb: h.activation(
                        out=upf[0][:, m2 * 2048 + nb * 512:
                                   m2 * 2048 + (nb + 1) * 512],
                        in_=pb[m2 * 2 + nb][:], func=AF.Copy,
                        scale=sc_d[:, m2:m2 + 1]),
                       reads=(f"pb{m2 * 2 + nb}", "sc_d"),
                       writes=("upf0",))
            layernorm([upf[0][:, m2 * 2048:m2 * 2048 + D] for m2 in range(2)],
                      ["upf0", "upf0"],
                      [upf[0][:, m2 * 2048:m2 * 2048 + D] for m2 in range(2)],
                      ["upf0", "upf0"], ST2, "st2", upf[1][:, 0:D], "upf1")
            for m2 in range(2):
                OP("v", lambda h, m2=m2, xb=xb: h.tensor_tensor(
                    out=xb[m2][:], in0=xb[m2][:],
                    in1=upf[0][:, m2 * 2048:m2 * 2048 + D], op=ALU.add),
                   reads=(xbn[m2], "upf0"), writes=(xbn[m2],))
                dma("q", out_d[c * LC + m2 * P:c * LC + (m2 + 1) * P, :],
                    xb[m2][:], reads=(xbn[m2],), writes=(f"out{c}_{m2}",))

        emit_front_p1(0)
        emit_front_p2(0)
        for c in range(NCH):
            emit_scan(c, 0, 3)
            if c > 1:
                emit_ffnC_kh(c - 2, 0)
            qstep(4)
            emit_scan(c, 3, 6)
            if c > 1:
                emit_ffnC_kh(c - 2, 1)
            qstep(4 if c == 0 else 99)
            emit_scan(c, 6, 8)
            if c > 1:
                emit_ffnC_fin(c - 2)
            if c > 0:
                emit_ffnA_mms(c - 1, 0)
            qstep(4)
            emit_scan(c, 8, 10)
            if c > 0:
                emit_ffnA_evacs(c - 1, 0)
                emit_ffnA_mms(c - 1, 1)
            qstep(4)
            emit_scan(c, 10, 12)
            if c > 0:
                emit_ffnA_evacs(c - 1, 1)
                emit_ffnA_mms(c - 1, 2)
            qstep(4)
            emit_scan(c, 12, 14)
            if c > 0:
                emit_ffnA_evacs(c - 1, 2)
                emit_ffnA_mms(c - 1, 3)
            qstep(4)
            emit_scan(c, 14, NST - 1)
            if c > 0:
                emit_ffnA_evacs(c - 1, 3)
            qstep(4)
            emit_tail(c)
            if c > 0:
                prog.stashing = True
                emit_ffnB(c - 1)
                prog.stashing = False
            emit_tail2(c)
            if c + 1 < NCH:
                emit_front_p1(c + 1)
            if c + 1 < NCH:
                emit_front_p2(c + 1)
            prog.drain()
        emit_ffnC_kh(NCH - 2, 0)
        emit_ffnC_kh(NCH - 2, 1)
        emit_ffnC_fin(NCH - 2)
        emit_ffnA(NCH - 1)
        emit_ffnB(NCH - 1)
        emit_ffnC_kh(NCH - 1, 0)
        emit_ffnC_kh(NCH - 1, 1)
        emit_ffnC_fin(NCH - 1)

        block = es.enter_context(nc.Block())
        block.tensor(prog.runner("t"))
        block.vector(prog.runner("v"))
        block.scalar(prog.runner("s"))
        block.sync(prog.runner("q"))
        block.gpsimd(prog.runner("g"))

    return nc


def kernel(L0_attn_x, L0_attn_in_proj_w, L0_attn_in_proj_b, L0_attn_conv_w,
           L0_attn_conv_b, L0_attn_bc_w, L0_attn_bc_b, L0_attn_dt_w,
           L0_attn_dt_b, L0_attn_A_log, L0_attn_D, L0_attn_out_w,
           L0_attn_out_b, L0_attn_ln_g, L0_attn_ln_b, L0_ffn_w_gate,
           L0_ffn_w_up, L0_ffn_w_down):
    f32 = np.float32
    c = np.ascontiguousarray

    x = c(np.asarray(L0_attn_x, dtype=f32))
    ipwT = c(np.asarray(L0_attn_in_proj_w, dtype=f32).T)
    ipb_c = c(np.asarray(L0_attn_in_proj_b, dtype=f32).reshape(16, P).T)
    convw_c = c(np.asarray(L0_attn_conv_w, dtype=f32)[:, 0, :]
                .reshape(NDB, P, 4).transpose(1, 0, 2).reshape(P, 32))
    convb_c = c(np.asarray(L0_attn_conv_b, dtype=f32).reshape(NDB, P).T)
    bcwT = c(np.asarray(L0_attn_bc_w, dtype=f32).T)
    bcb_c = c(np.asarray(L0_attn_bc_b, dtype=f32).reshape(32, 1))
    dtwT = c(np.asarray(L0_attn_dt_w, dtype=f32).T)
    dtb_c = c(np.asarray(L0_attn_dt_b, dtype=f32).reshape(NDB, P).T)
    alog_c = c(np.asarray(L0_attn_A_log, dtype=f32)
               .reshape(NDB, P, NST).transpose(1, 0, 2).reshape(P, P))
    d_c = c(np.asarray(L0_attn_D, dtype=f32).reshape(NDB, P).T)
    outwT = c(np.asarray(L0_attn_out_w, dtype=f32).T)
    outb_r = c(np.tile(np.asarray(L0_attn_out_b, dtype=f32)[None, :], (P, 1)))
    lng_r = c(np.tile(np.asarray(L0_attn_ln_g, dtype=f32)[None, :], (P, 1)))
    lnb_r = c(np.tile(np.asarray(L0_attn_ln_b, dtype=f32)[None, :], (P, 1)))
    wg = c(np.asarray(L0_ffn_w_gate, dtype=f32))
    wu = c(np.asarray(L0_ffn_w_up, dtype=f32))
    wd = c(np.asarray(L0_ffn_w_down, dtype=f32))
    ident = np.eye(P, dtype=f32)
    onescol = np.ones((P, 1), dtype=f32)
    consts = np.zeros((P, 8), dtype=f32)
    consts[:, C_EPS] = EPS
    consts[:, C_NHALF] = -0.5
    consts[:, C_PHALF] = 0.5
    consts[:, C_NHALFSC] = -0.5
    consts[:, C_127] = 127.0

    nc = build_nc()
    shared = {"ipwT": ipwT, "ipb_c": ipb_c, "convw_c": convw_c,
              "convb_c": convb_c, "bcwT": bcwT, "bcb_c": bcb_c, "dtwT": dtwT,
              "dtb_c": dtb_c, "alog_c": alog_c, "d_c": d_c, "outwT": outwT,
              "outb_r": outb_r, "lng_r": lng_r, "lnb_r": lnb_r, "wg": wg,
              "wu": wu, "wd": wd, "ident": ident, "onescol": onescol,
              "consts": consts}
    in_maps = [dict(shared, x=c(x[b])) for b in range(8)]
    res = run_bass_kernel_spmd(nc, in_maps, core_ids=list(range(8)))
    return np.stack([np.asarray(r["out"], dtype=f32) for r in res.results])


# version 19
# speedup vs baseline: 1.0459x; 1.0459x over previous
import contextlib
import numpy as np
import concourse.bass as bass
import concourse.mybir as mybir
from concourse.bass_utils import run_bass_kernel_spmd

F32 = mybir.dt.float32
I32 = mybir.dt.int32
BF16 = mybir.dt.bfloat16
FP8 = mybir.dt.float8e4
AF = mybir.ActivationFunctionType
ALU = mybir.AluOpType
AX = mybir.AxisListType

P = 128
L = 2048
D = 1024
DI = 4096
NST = 16
LC = 256
NCH = L // LC
NDB = D // P
EPS = 1e-5

C_EPS, C_NHALF, C_PHALF, C_NHALFSC, C_127 = 0, 1, 2, 3, 4


class Prog:
    STREAM = {"t": "t", "v": "v", "s": "s", "q": "q", "g": "g", "r": "s"}
    QPOOL = {"q": 16, "g": 8, "r": 4}

    def __init__(self, nc, es):
        self.sems = {e: es.enter_context(nc.semaphore(f"sem_{e}"))
                     for e in ("t", "v", "s")}
        for e, K in self.QPOOL.items():
            for p in range(K):
                self.sems[f"{e}{p}"] = es.enter_context(
                    nc.semaphore(f"sem_{e}{p}"))
        self.cnt = {e: 0 for e in ("t", "v", "s")}
        self.use = {e: 0 for e in self.QPOOL}
        self.ops = {e: [] for e in ("t", "v", "s", "q", "g")}
        self.wtok = {}
        self.rtok = {}
        self.waited = {}
        self.stash_q = []
        self.stashing = False

    def drain(self, n=None):
        take = len(self.stash_q) if n is None else min(n, len(self.stash_q))
        items = self.stash_q[:take]
        del self.stash_q[:take]
        for eng, emit, reads, writes in items:
            self.op(eng, emit, reads, writes)

    def op(self, eng, emit, reads=(), writes=()):
        if self.stashing:
            self.stash_q.append((eng, emit, reads, writes))
            return None
        waits = {}

        def addw(tok):
            if tok is None:
                return
            sn, v = tok
            if sn == "t" and eng == "t":
                return
            if v > waits.get(sn, 0):
                waits[sn] = v

        for b in reads:
            addw(self.wtok.get(b))
        for b in writes:
            addw(self.wtok.get(b))
            for sn, v in self.rtok.get(b, {}).items():
                addw((sn, v))
        if eng in self.QPOOL:
            K = self.QPOOL[eng]
            i = self.use[eng]
            self.use[eng] += 1
            sn = f"{eng}{i % K}"
            n = i // K + 1
            if n > 1 and waits.get(sn, 0) < 16 * (n - 1):
                waits[sn] = 16 * (n - 1)
            tok = (sn, 16 * n)
        else:
            self.cnt[eng] += 1
            tok = (eng, self.cnt[eng])
        wcache = self.waited.setdefault(eng, {})
        final = {}
        for sn, v in waits.items():
            if wcache.get(sn, 0) >= v:
                continue
            final[sn] = v
            wcache[sn] = v
        self.ops[self.STREAM[eng]].append((final, emit, tok))
        for b in writes:
            self.wtok[b] = tok
            self.rtok[b] = {}
        for b in reads:
            d = self.rtok.setdefault(b, {})
            if tok[1] > d.get(tok[0], 0):
                d[tok[0]] = tok[1]
        return tok

    def runner(self, eng):
        sems, ops = self.sems, self.ops[eng]
        finals = []
        if eng == "q":
            for e, K in self.QPOOL.items():
                for p in range(K):
                    uses = (self.use[e] - p + K - 1) // K
                    if uses > 0:
                        finals.append((f"{e}{p}", 16 * uses))

        def run(h):
            for waits, emit, tok in ops:
                for sn in sorted(waits):
                    h.wait_ge(sems[sn], waits[sn])
                inst = emit(h)
                if inst is not None:
                    inst.then_inc(sems[tok[0]], 16 if len(tok[0]) > 1 else 1)
            for sn, v in finals:
                h.wait_ge(sems[sn], v)
        return run


def build_nc():
    nc = bass.Bass("TRN2", target_bir_lowering=False)

    def inp(name, shape, dtype=F32):
        return nc.dram_tensor(name, shape, dtype, kind="ExternalInput").ap()

    x_d = inp("x", [L, D])
    ipwT_d = inp("ipwT", [D, 2 * D])
    ipb_d = inp("ipb_c", [P, 16])
    convw_d = inp("convw_c", [P, 32])
    convb_d = inp("convb_c", [P, 8])
    bcwT_d = inp("bcwT", [D, 32])
    bcb_d = inp("bcb_c", [32, 1])
    dtwT_d = inp("dtwT", [D, D])
    dtb_d = inp("dtb_c", [P, 8])
    alog_d = inp("alog_c", [P, P])
    dcol_d = inp("d_c", [P, 8])
    outwT_d = inp("outwT", [D, D])
    outb_d = inp("outb_r", [P, D])
    lng_d = inp("lng_r", [P, D])
    lnb_d = inp("lnb_r", [P, D])
    wg_d = inp("wg", [DI, D])
    wu_d = inp("wu", [DI, D])
    wd_d = inp("wd", [D, DI])
    ident_d = inp("ident", [P, P])
    ones_d = inp("onescol", [P, 1])
    consts_d = inp("consts", [P, 8])

    out_d = nc.dram_tensor("out", [L, D], F32, kind="ExternalOutput").ap()

    swgT_d = nc.dram_tensor("swgT", [D, DI], FP8, kind="Internal").ap()
    swuT_d = nc.dram_tensor("swuT", [D, DI], FP8, kind="Internal").ap()
    swdT_d = nc.dram_tensor("swdT", [DI, D], FP8, kind="Internal").ap()
    sbc_d = nc.dram_tensor("sbc", [32, LC], F32, kind="Internal").ap()
    ssc_d = nc.dram_tensor("ssc", [1, 8], F32, kind="Internal").ap()
    sip_d = nc.dram_tensor("sip", [D, 2 * D], BF16, kind="Internal").ap()
    sdt_d = nc.dram_tensor("sdt", [D, D], BF16, kind="Internal").ap()
    sout_d = nc.dram_tensor("sout", [D, D], BF16, kind="Internal").ap()

    es = contextlib.ExitStack()
    with es:
        def sb(name, shape, dtype=F32):
            return es.enter_context(nc.sbuf_tensor(name, shape, dtype))

        ipb = sb("ipb", [P, 16]); convw = sb("convw", [P, 32])
        convb = sb("convb", [P, 8]); bcb = sb("bcb", [32, 1])
        dtb = sb("dtb", [P, 8]); alog = sb("alog", [P, P])
        a_t = sb("a_t", [P, P]); dcol = sb("dcol", [P, 8])
        outb = sb("outb", [P, D]); lng = sb("lng", [P, D]); lnb = sb("lnb", [P, D])
        ident = sb("identsb", [P, P]); ones = sb("onessb", [P, 1])
        consts = sb("constssb", [P, 8])
        bcwT = sb("bcwTsb", [P, NDB * 32], BF16)
        carry = sb("carry", [P, P])
        mcg = sb("mcg", [P, 1]); mcu = sb("mcu", [P, 1]); mcd = sb("mcd", [P, 1])
        swg_c = sb("swg_c", [P, 1]); swu_c = sb("swu_c", [P, 1])
        swd_c = sb("swd_c", [P, 1])
        qsc = sb("qsc", [1, 8])
        accq = sb("accq", [P, 32])

        st_sum = sb("st_sum", [P, 2]); st_ssq = sb("st_ssq", [P, 2])
        st_mean = sb("st_mean", [P, 2]); st_var = sb("st_var", [P, 2])
        st_rstd = sb("st_rstd", [P, 2]); st_nmr = sb("st_nmr", [P, 2])
        st_tmp = sb("st_tmp", [P, 2])
        s2_sum = sb("s2_sum", [P, 2]); s2_ssq = sb("s2_ssq", [P, 2])
        s2_mean = sb("s2_mean", [P, 2]); s2_var = sb("s2_var", [P, 2])
        s2_rstd = sb("s2_rstd", [P, 2]); s2_nmr = sb("s2_nmr", [P, 2])
        s2_tmp = sb("s2_tmp", [P, 2])
        fx_mx = sb("fx_mx", [P, 2]); fx_sx = sb("fx_sx", [P, 2])
        fx_sxi = sb("fx_sxi", [P, 2])
        fz_mx = sb("fz_mx", [P, 2]); fz_sz = sb("fz_sz", [P, 2])
        fz_szi = sb("fz_szi", [P, 2])
        sc_g = sb("sc_g", [P, 2]); sc_u = sb("sc_u", [P, 2]); sc_d = sb("sc_d", [P, 2])

        xbuf = [[sb(f"xbuf{par}_{i}", [P, D]) for i in range(2)]
                for par in range(3)]
        xzn = [sb(f"xzn{i}", [P, D]) for i in range(2)]
        xnT = sb("xnT", [P, NDB * LC], BF16)
        xpad = sb("xpad", [P, NDB * (LC + 4)])
        xcv = sb("xcv", [P, NDB * LC])
        sgt = sb("sgt", [P, NDB * LC])
        dtt = sb("dtt", [P, NDB * LC])
        bct = sb("bct", [32, LC])
        bcc = [sb(f"bcc{i}", [P, LC]) for i in range(4)]
        pool6 = [sb(f"pool6_{i}", [P, LC]) for i in range(13)]
        wstr = [sb(f"wstr{i}", [P, D], BF16) for i in range(2)]

        zbuf2 = [sb(f"zbuf2_{i}", [P, DI]) for i in range(2)]
        upf = [sb(f"upf{i}", [P, DI]) for i in range(2)]
        rscr = sb("rscr", [P, 2048])
        ffci = sb("ffci", [P, 2048], I32)
        ffci2 = sb("ffci2", [P, LC], I32)
        xqT = sb("xqT", [P, NDB * 256], BF16)
        zqT = sb("zqT", [P, DI], BF16)
        wsg = [sb(f"wsg{i}", [P, 2048], FP8) for i in range(2)]
        wsd = [sb(f"wsd{i}", [P, 2048], FP8) for i in range(2)]

        pb = [es.enter_context(nc.psum_tensor(f"pb{i}", [P, 512], F32))
              for i in range(8)]

        prog = Prog(nc, es)
        OP = prog.op

        def dma(queue, dst, src, reads, writes):
            return OP(queue, lambda h, d=dst, s=src: h.dma_start(out=d, in_=s),
                      reads=reads, writes=writes)

        for (t, d_, nm) in [(ipb, ipb_d, "ipb"), (convw, convw_d, "convw"),
                            (convb, convb_d, "convb"), (bcb, bcb_d, "bcb"),
                            (dtb, dtb_d, "dtb"), (alog, alog_d, "alog"),
                            (dcol, dcol_d, "dcol"), (outb, outb_d, "outb"),
                            (lng, lng_d, "lng"), (lnb, lnb_d, "lnb"),
                            (ident, ident_d, "ident"), (ones, ones_d, "ones"),
                            (consts, consts_d, "consts")]:
            dma("q", t[:], d_, reads=(), writes=(nm,))
        for k in range(NDB):
            dma("g", bcwT[:, k * 32:(k + 1) * 32], bcwT_d[k * P:(k + 1) * P, :],
                reads=(), writes=("bcwT",))
        OP("s", lambda h: h.activation(out=a_t[:], in_=alog[:], func=AF.Exp),
           reads=("alog",), writes=("a_t",))
        OP("v", lambda h: h.tensor_scalar(out=a_t[:], in0=a_t[:], scalar1=-1.0,
                                          scalar2=None, op0=ALU.mult),
           reads=("a_t",), writes=("a_t",))
        OP("v", lambda h: h.memset(carry[:], 0.0), reads=(), writes=("carry",))

        for (w_d, s_d, cols, tag) in ((ipwT_d, sip_d, 2 * D, "ip"),
                                      (dtwT_d, sdt_d, D, "dt"),
                                      (outwT_d, sout_d, D, "out")):
            for k in range(NDB):
                for half in range(cols // D):
                    wb = wstr[k % 2]
                    dma("g", wb[:], w_d[k * P:(k + 1) * P,
                                        half * D:(half + 1) * D],
                        reads=(), writes=(f"wstr{k % 2}",))
                    dma("q", s_d[k * P:(k + 1) * P, half * D:(half + 1) * D],
                        wb[:], reads=(f"wstr{k % 2}",), writes=(f"smw_{tag}",))

        qrot = [0]

        def quant_matrix(tag, w_d, rows, cols, mc_col, sw_col, sT_d, mi):
            ngr = rows * cols // (P * 4096)
            per = 4096 // cols
            ncb = cols // P
            ldb = [zbuf2[0][:, 0:4096], zbuf2[1][:, 0:4096]]
            ldn = ["zbuf2_0", "zbuf2_1"]
            s1b = [upf[0][:, 0:2048], upf[0][:, 2048:4096]]
            s1n = ["upf0A", "upf0B"]
            s2b = [upf[1][:, 0:2048], upf[1][:, 2048:4096]]
            s2n = ["upf1A", "upf1B"]
            q3b = [rscr[:, 0:2048], xcv[:, 0:2048]]
            q3n = ["rscrA", "xcvA"]

            def load(g_):
                lb, ln_ = ldb[g_ % 2], ldn[g_ % 2]
                src_ = w_d[g_ * per * P:(g_ + 1) * per * P, :]
                if per > 1:
                    src_ = src_.rearrange("(a b) cc -> b a cc", b=P)
                    dst_ = lb.rearrange("z (a cc) -> z a cc", a=per)
                else:
                    dst_ = lb
                dma("r", dst_, src_, reads=(), writes=(ln_,))
                return lb, ln_

            for g_ in range(ngr):
                lb, ln_ = load(g_)
                for hf in range(2):
                    OP("s", lambda h, lb=lb, g_=g_, hf=hf, mi=mi:
                       h.activation(
                           out=s1b[hf], in_=lb[:, hf * 2048:(hf + 1) * 2048],
                           func=AF.Abs,
                           accum_out=accq[:, (mi % 2) * 16 + 2 * g_ + hf:
                                          (mi % 2) * 16 + 2 * g_ + hf + 1]),
                       reads=(ln_,), writes=(s1n[hf], f"accq{mi}"))
            OP("t", lambda h, ngr=ngr, mi=mi: h.matmul(
                pb[0][0:1, 0:2 * ngr], ones[:, 0:1],
                accq[:, (mi % 2) * 16:(mi % 2) * 16 + 2 * ngr],
                start=True, stop=True),
               reads=(f"accq{mi}", "ones"), writes=("pb0",))
            OP("v", lambda h, ngr=ngr, mi=mi: h.tensor_reduce(
                out=qsc[0:1, mi:mi + 1], in_=pb[0][0:1, 0:2 * ngr], axis=AX.X,
                op=ALU.add), reads=("pb0",), writes=("qsc",))
            OP("v", lambda h, mi=mi: h.tensor_scalar(
                out=qsc[0:1, mi:mi + 1], in0=qsc[0:1, mi:mi + 1],
                scalar1=1.0 / (rows * cols), scalar2=1e-5, op0=ALU.mult,
                op1=ALU.max), reads=("qsc",), writes=("qsc",))
            dma("q", ssc_d[0:1, mi:mi + 1], qsc[0:1, mi:mi + 1],
                reads=("qsc",), writes=(f"ssc{mi}",))
            dma("q", mc_col[:], ssc_d[0:1, mi:mi + 1].broadcast_to((P, 1)),
                reads=(f"ssc{mi}",), writes=(f"mc_{tag}",))
            OP("v", lambda h: h.reciprocal(out=sw_col[:], in_=mc_col[:]),
               reads=(f"mc_{tag}",), writes=(f"sw_{tag}",))
            for g_ in range(ngr):
                lb, ln_ = load(g_)
                for hf in range(2):
                    hsl = slice(hf * 2048, (hf + 1) * 2048)
                    OP("s", lambda h, lb=lb, hsl=hsl, hf=hf: h.activation(
                        out=s1b[hf], in_=lb[:, hsl], func=AF.Sign,
                        scale=sw_col[:, 0:1],
                        bias=consts[:, C_NHALF:C_NHALF + 1]),
                       reads=(ln_, f"sw_{tag}", "consts"), writes=(s1n[hf],))
                    OP("s", lambda h, lb=lb, hsl=hsl, hf=hf: h.activation(
                        out=s2b[hf], in_=lb[:, hsl], func=AF.Sign,
                        scale=sw_col[:, 0:1],
                        bias=consts[:, C_PHALF:C_PHALF + 1]),
                       reads=(ln_, f"sw_{tag}", "consts"), writes=(s2n[hf],))
                    OP("v", lambda h, hf=hf: h.tensor_tensor(
                        out=q3b[hf], in0=s1b[hf], in1=s2b[hf], op=ALU.add),
                       reads=(s1n[hf], s2n[hf]), writes=(q3n[hf],))
                    for cb4 in range(4):
                        bank = pb[6 + cb4 % 2]
                        bknm = f"pb{6 + cb4 % 2}"
                        for j in range(4):
                            blk = hf * 16 + cb4 * 4 + j
                            a, cb = divmod(blk, ncb)
                            off = (a * cols + cb * P) - hf * 2048
                            OP("t", lambda h, off=off, j=j, bank=bank, hf=hf:
                               h.transpose(bank[:, j * P:(j + 1) * P],
                                           q3b[hf][:, off:off + P], ident[:]),
                               reads=(q3n[hf], "ident"), writes=(bknm,))
                        blk0 = hf * 16 + cb4 * 4
                        a0, cb0 = divmod(blk0, ncb)
                        ti0 = g_ * per + a0
                        r = qrot[0] % 4
                        qrot[0] += 1
                        OP("v", lambda h, bank=bank, r=r: h.tensor_copy(
                            out=wsg[0][:, r * 512:(r + 1) * 512],
                            in_=bank[:, 0:512]),
                           reads=(bknm,), writes=(f"wsg0_{r}",))
                        dma("q", sT_d[cb0 * P:cb0 * P + 512,
                                      ti0 * P:(ti0 + 1) * P]
                            .rearrange("(a b) cc -> b a cc", b=P),
                            wsg[0][:, r * 512:(r + 1) * 512]
                            .rearrange("z (a cc) -> z a cc", a=4),
                            reads=(f"wsg0_{r}",), writes=(f"sT_{tag}",))

        quant_matrix("wg", wg_d, DI, D, mcg, swg_c, swgT_d, 0)
        quant_matrix("wu", wu_d, DI, D, mcu, swu_c, swuT_d, 1)
        quant_matrix("wd", wd_d, D, DI, mcd, swd_c, swdT_d, 2)
        OP("v", lambda h: h.engine_nop(), reads=(),
           writes=tuple(f"xcv{i}" for i in range(NDB)) + tuple(
               f"zbq{m2}_{q}" for m2 in range(2) for q in range(4)) + (
               "upf0", "upf1", "rscr", "zbuf2_0", "zbuf2_1",
               "upf0A", "upf0B", "upf1A", "upf1B", "rscrA", "xcvA",
               "wsg0", "wsg0_0", "wsg0_1", "wsg0_2", "wsg0_3"))

        def layernorm(bufs, names, out_bufs, out_names, st, stn, scr, scrn):
            nb = len(bufs)
            scrns = scrn if isinstance(scrn, tuple) else (scrn,)
            (ssum, sssq, smean, svar, srstd, snmr, stmp) = st
            for i, (b, nm) in enumerate(zip(bufs, names)):
                OP("s", lambda h, b=b, i=i: h.activation(
                    out=scr, in_=b, func=AF.Square,
                    accum_out=sssq[:, i:i + 1]),
                   reads=(nm,), writes=scrns + (stn,))
                OP("s", lambda h, b=b, i=i: h.activation(
                    out=scr, in_=b, func=AF.Identity,
                    accum_out=ssum[:, i:i + 1]),
                   reads=(nm,), writes=scrns + (stn,))
            OP("v", lambda h, nb=nb: h.tensor_scalar(
                out=smean[:, 0:nb], in0=ssum[:, 0:nb], scalar1=1.0 / D,
                scalar2=None, op0=ALU.mult), reads=(stn,), writes=(stn,))
            OP("v", lambda h, nb=nb: h.tensor_scalar(
                out=svar[:, 0:nb], in0=sssq[:, 0:nb], scalar1=1.0 / D,
                scalar2=None, op0=ALU.mult), reads=(stn,), writes=(stn,))
            OP("v", lambda h, nb=nb: h.tensor_tensor(
                out=stmp[:, 0:nb], in0=smean[:, 0:nb], in1=smean[:, 0:nb],
                op=ALU.mult), reads=(stn,), writes=(stn,))
            OP("v", lambda h, nb=nb: h.tensor_tensor(
                out=svar[:, 0:nb], in0=svar[:, 0:nb], in1=stmp[:, 0:nb],
                op=ALU.subtract), reads=(stn,), writes=(stn,))
            OP("s", lambda h, nb=nb: h.activation(
                out=srstd[:, 0:nb], in_=svar[:, 0:nb], func=AF.Ln,
                bias=consts[:, C_EPS:C_EPS + 1]),
               reads=(stn, "consts"), writes=(stn,))
            OP("s", lambda h, nb=nb: h.activation(
                out=srstd[:, 0:nb], in_=srstd[:, 0:nb], func=AF.Exp,
                scale=consts[:, C_NHALFSC:C_NHALFSC + 1]),
               reads=(stn, "consts"), writes=(stn,))
            OP("v", lambda h, nb=nb: h.scalar_tensor_tensor(
                out=snmr[:, 0:nb], in0=smean[:, 0:nb], scalar=-1.0,
                in1=srstd[:, 0:nb], op0=ALU.mult, op1=ALU.mult),
               reads=(stn,), writes=(stn,))
            for i, (b, ob, onm) in enumerate(zip(bufs, out_bufs, out_names)):
                OP("s", lambda h, b=b, ob=ob, i=i: h.activation(
                    out=ob, in_=b, func=AF.Identity,
                    scale=srstd[:, i:i + 1], bias=snmr[:, i:i + 1]),
                   reads=(names[i], stn), writes=(onm,))

        ST1 = (st_sum, st_ssq, st_mean, st_var, st_rstd, st_nmr, st_tmp)
        ST2 = (s2_sum, s2_ssq, s2_mean, s2_var, s2_rstd, s2_nmr, s2_tmp)

        def round_chain(t_ap, t_nm, ci_ap, ci_nm, c1_ap, c1_nm, dp_ap, dp_nm,
                        out_ap, out_nm, rng=1, collect=False):
            ops = []
            OP = (lambda e, f, reads, writes:
                  ops.append((e, f, reads, writes))) if collect else prog.op
            OP("s", lambda h: h.activation(out=ci_ap, in_=t_ap, func=AF.Copy),
               reads=(t_nm,), writes=(ci_nm,))
            OP("v", lambda h: h.tensor_scalar(
                out=c1_ap, in0=ci_ap, scalar1=-float(rng), scalar2=None,
                op0=ALU.add),
               reads=(ci_nm,), writes=(c1_nm,))
            OP("v", lambda h: h.tensor_tensor(
                out=dp_ap, in0=t_ap, in1=c1_ap, op=ALU.subtract),
               reads=(t_nm, c1_nm), writes=(dp_nm,))
            for j in range(2 * rng):
                alu = ALU.is_ge if j == rng - 1 else ALU.is_gt
                dst_ap = out_ap if j == 2 * rng - 1 else c1_ap
                dst_nm = out_nm if j == 2 * rng - 1 else c1_nm
                OP("v", lambda h, j=j, alu=alu, dst_ap=dst_ap:
                   h.scalar_tensor_tensor(
                       out=dst_ap, in0=dp_ap, scalar=j + 0.5, in1=c1_ap,
                       op0=alu, op1=ALU.add),
                   reads=(dp_nm, c1_nm), writes=(dst_nm,))
            return ops

        def emit_front_p1(c):
            xb = xbuf[c % 3]
            xbn = [f"xbuf{c % 3}_{i}" for i in range(2)]
            for m in range(2):
                dma("q", xb[m][:], x_d[c * LC + m * P:c * LC + (m + 1) * P, :],
                    reads=(), writes=(xbn[m],))
            layernorm([xb[m][:] for m in range(2)], xbn,
                      [xzn[m][:] for m in range(2)],
                      [f"xzn{m}" for m in range(2)], ST1, "st1",
                      sgt[:, 0:D], ("sgt0", "sgt1", "sgt2", "sgt3"))
            prog.drain(2)
            for k in range(NDB):
                bank, bknm = pb[6 + k % 2], f"pb{6 + k % 2}"
                for m in range(2):
                    OP("t", lambda h, k=k, m=m, bank=bank: h.transpose(
                        bank[:, m * P:(m + 1) * P],
                        xzn[m][:, k * P:(k + 1) * P], ident[:]),
                       reads=(f"xzn{m}", "ident"), writes=(bknm,))
                OP("s", lambda h, k=k, bank=bank: h.activation(
                    out=xnT[:, k * LC:(k + 1) * LC], in_=bank[:, 0:LC],
                    func=AF.Copy), reads=(bknm,), writes=(f"xnT{k}",))
                if k % 2 == 1:
                    prog.drain(1)
            for db in range(NDB):
                o = db * (LC + 4)
                if c == 0:
                    OP("v", lambda h, o=o: h.memset(xpad[:, o:o + 3], 0.0),
                       reads=(f"xpad{db}",), writes=(f"xpad{db}",))
                else:
                    OP("v", lambda h, o=o: h.tensor_copy(
                        out=xpad[:, o:o + 3], in_=xpad[:, o + LC:o + LC + 3]),
                       reads=(f"xpad{db}",), writes=(f"xpad{db}",))
            for half in range(2):
                for k in range(NDB):
                    wb = wstr[k % 2]
                    dma("q", wb[:], sip_d[k * P:(k + 1) * P,
                                          half * D:(half + 1) * D],
                        reads=("smw_ip",), writes=(f"wstr{k % 2}",))
                    for m in range(8):
                        OP("t", lambda h, wb=wb, k=k, m=m: h.matmul(
                            pb[m][:, 0:LC], wb[:, m * P:(m + 1) * P],
                            xnT[:, k * LC:(k + 1) * LC],
                            start=(k == 0), stop=(k == NDB - 1)),
                           reads=(f"wstr{k % 2}", f"xnT{k}"),
                           writes=(f"pb{m}",))
                for m in range(8):
                    mt = half * 8 + m
                    if half == 0:
                        o = m * (LC + 4)
                        OP("s", lambda h, m=m, mt=mt, o=o: h.activation(
                            out=xpad[:, o + 3:o + 3 + LC], in_=pb[m][:, 0:LC],
                            func=AF.Identity, bias=ipb[:, mt:mt + 1]),
                           reads=(f"pb{m}", "ipb", f"xpad{m}"),
                           writes=(f"xpad{m}",))
                    else:
                        OP("s", lambda h, m=m, mt=mt: h.activation(
                            out=sgt[:, m * LC:(m + 1) * LC],
                            in_=pb[m][:, 0:LC],
                            func=AF.Silu, bias=ipb[:, mt:mt + 1]),
                           reads=(f"pb{m}", "ipb"), writes=(f"sgt{m}",))
                    if m % 2 == 1:
                        prog.drain(2)
        def emit_front_p2(c):
            for dp in range(NDB // 2):
                pair = (2 * dp, 2 * dp + 1)
                acc = {pair[0]: (pool6[0], "p6_0"), pair[1]: (pool6[3], "p6_3")}
                for db in pair:
                    o = db * (LC + 4)
                    ab, an = acc[db]
                    OP("v", lambda h, o=o, db=db, ab=ab: h.tensor_scalar(
                        out=ab[:], in0=xpad[:, o:o + LC],
                        scalar1=convw[:, db * 4:db * 4 + 1], scalar2=None,
                        op0=ALU.mult),
                       reads=(f"xpad{db}", "convw"), writes=(an,))
                for k in range(1, 4):
                    for db in pair:
                        o = db * (LC + 4)
                        ab, an = acc[db]
                        OP("v", lambda h, o=o, db=db, k=k, ab=ab:
                           h.scalar_tensor_tensor(
                               out=ab[:], in0=xpad[:, o + k:o + k + LC],
                               scalar=convw[:, db * 4 + k:db * 4 + k + 1],
                               in1=ab[:], op0=ALU.mult, op1=ALU.add),
                           reads=(f"xpad{db}", "convw", an), writes=(an,))
                for db in pair:
                    ab, an = acc[db]
                    OP("s", lambda h, db=db, ab=ab: h.activation(
                        out=xcv[:, db * LC:(db + 1) * LC], in_=ab[:],
                        func=AF.Silu, bias=convb[:, db:db + 1]),
                       reads=(an, "convb"), writes=(f"xcv{db}",))
                    OP("s", lambda h, db=db, ab=ab: h.activation(
                        out=xnT[:, db * LC:(db + 1) * LC], in_=ab[:],
                        func=AF.Silu, bias=convb[:, db:db + 1]),
                       reads=(an, "convb"), writes=(f"xnT{db}",))
                prog.drain(2)
            for k in range(NDB):
                OP("t", lambda h, k=k: h.matmul(
                    pb[6][0:32, 0:LC], bcwT[:, k * 32:(k + 1) * 32],
                    xnT[:, k * LC:(k + 1) * LC],
                    start=(k == 0), stop=(k == NDB - 1)),
                   reads=("bcwT", f"xnT{k}"), writes=("pb6",))
            OP("s", lambda h: h.activation(out=bct[:], in_=pb[6][0:32, 0:LC],
                                           func=AF.Identity, bias=bcb[:, 0:1]),
               reads=("pb6", "bcb"), writes=("bct",))
            dma("q", sbc_d[:, :], bct[:], reads=("bct",), writes=("sbc",))
            for k in range(NDB):
                wb = wstr[k % 2]
                dma("q", wb[:], sdt_d[k * P:(k + 1) * P, :],
                    reads=("smw_dt",), writes=(f"wstr{k % 2}",))
                for m in range(8):
                    OP("t", lambda h, wb=wb, k=k, m=m: h.matmul(
                        pb[m][:, 0:LC], wb[:, m * P:(m + 1) * P],
                        xnT[:, k * LC:(k + 1) * LC],
                        start=(k == 0), stop=(k == NDB - 1)),
                       reads=(f"wstr{k % 2}", f"xnT{k}"), writes=(f"pb{m}",))
                prog.drain(2)
            for m in range(8):
                OP("s", lambda h, m=m: h.activation(
                    out=pool6[1][:], in_=pb[m][:, 0:LC], func=AF.Abs,
                    bias=dtb[:, m:m + 1]),
                   reads=(f"pb{m}", "dtb"), writes=("p6_1",))
                OP("s", lambda h: h.activation(
                    out=pool6[1][:], in_=pool6[1][:], func=AF.Exp,
                    scale=-1.0),
                   reads=("p6_1",), writes=("p6_1",))
                OP("s", lambda h: h.activation(
                    out=pool6[1][:], in_=pool6[1][:], func=AF.Ln, bias=1.0),
                   reads=("p6_1",), writes=("p6_1",))
                OP("s", lambda h, m=m: h.activation(
                    out=pool6[2][:], in_=pb[m][:, 0:LC], func=AF.Relu,
                    bias=dtb[:, m:m + 1]),
                   reads=(f"pb{m}", "dtb"), writes=("p6_2",))
                OP("v", lambda h, m=m: h.tensor_tensor(
                    out=dtt[:, m * LC:(m + 1) * LC], in0=pool6[1][:],
                    in1=pool6[2][:], op=ALU.add),
                   reads=("p6_1", "p6_2"), writes=(f"dtt{m}",))
            for db in range(NDB):
                o = db * (LC + 4)
                OP("v", lambda h, db=db, o=o: h.tensor_tensor(
                    out=xpad[:, o:o + LC], in0=dtt[:, db * LC:(db + 1) * LC],
                    in1=xcv[:, db * LC:(db + 1) * LC], op=ALU.mult),
                   reads=(f"dtt{db}", f"xcv{db}", f"xpad{db}"),
                   writes=(f"xpad{db}",))
            for db in range(NDB):
                OP("s", lambda h, db=db: h.activation(
                    out=xcv[:, db * LC:(db + 1) * LC],
                    in_=xcv[:, db * LC:(db + 1) * LC], func=AF.Copy,
                    scale=dcol[:, db:db + 1]),
                   reads=(f"xcv{db}", "dcol"), writes=(f"xcv{db}",))

        pend_carry = []

        def push_carry(hh, hhnm, col):
            pend_carry.append((hh, hhnm, col))

        def flush_carry(k=None):
            take = len(pend_carry) if k is None else k
            for _ in range(take):
                hh, hhnm, col = pend_carry.pop(0)
                OP("s", lambda h, hh=hh, col=col: h.activation(
                    out=carry[:, col:col + 1], in_=hh[:, LC - 1:LC],
                    func=AF.Copy),
                   reads=(hhnm,), writes=("carry",))

        def scan_group(c, n, g, bB, bBn, bC, bCn, tail=False):
            flush_carry()
            dbs = range(g * 4, g * 4 + 4)

            def bufs(db):
                i = n * NDB + db
                da, danm = pool6[i % 4], f"p6_{i % 4}"
                bb, bbnm = pool6[4 + i % 4], f"p6_{4 + i % 4}"
                hx = 8 + i % 5
                hh, hhnm = pool6[hx], f"p6_{hx}"
                return da, danm, bb, bbnm, hh, hhnm, db * NST + n, db * (LC + 4)

            for db in dbs:
                da, danm, bb, bbnm, hh, hhnm, col, o = bufs(db)
                OP("s", lambda h, da=da, db=db, col=col: h.activation(
                    out=da[:], in_=dtt[:, db * LC:(db + 1) * LC],
                    func=AF.Exp, scale=a_t[:, col:col + 1]),
                   reads=(f"dtt{db}", "a_t"), writes=(danm,))
            for db in dbs:
                da, danm, bb, bbnm, hh, hhnm, col, o = bufs(db)
                OP("v", lambda h, bb=bb, o=o, bB=bB: h.tensor_tensor(
                    out=bb[:], in0=xpad[:, o:o + LC], in1=bB[:],
                    op=ALU.mult),
                   reads=(f"xpad{db}", bBn), writes=(bbnm,))
            for db in dbs:
                da, danm, bb, bbnm, hh, hhnm, col, o = bufs(db)
                OP("v", lambda h, hh=hh, da=da, bb=bb, col=col:
                   h.tensor_tensor_scan(
                       out=hh[:], data0=da[:], data1=bb[:],
                       initial=carry[:, col:col + 1],
                       op0=ALU.mult, op1=ALU.add),
                   reads=(danm, bbnm, "carry"), writes=(hhnm,))
            for db in dbs:
                da, danm, bb, bbnm, hh, hhnm, col, o = bufs(db)
                push_carry(hh, hhnm, col)
            for db in dbs:
                da, danm, bb, bbnm, hh, hhnm, col, o = bufs(db)
                OP("v", lambda h, hh=hh, bb=bb, bC=bC: h.tensor_tensor(
                    out=bb[:], in0=hh[:], in1=bC[:], op=ALU.mult),
                   reads=(hhnm, bCn), writes=(bbnm,))
            for db in dbs:
                da, danm, bb, bbnm, hh, hhnm, col, o = bufs(db)
                OP("v", lambda h, bb=bb, db=db: h.tensor_tensor(
                    out=xcv[:, db * LC:(db + 1) * LC], in0=bb[:],
                    in1=xcv[:, db * LC:(db + 1) * LC], op=ALU.add),
                   reads=(bbnm, f"xcv{db}"), writes=(f"xcv{db}",))
            if tail:
                for db in dbs:
                    OP("v", lambda h, db=db: h.tensor_tensor(
                        out=xnT[:, db * LC:(db + 1) * LC],
                        in0=xcv[:, db * LC:(db + 1) * LC],
                        in1=sgt[:, db * LC:(db + 1) * LC], op=ALU.mult),
                       reads=(f"xcv{db}", f"sgt{db}"), writes=(f"xnT{db}",))
                for db in dbs:
                    k = db
                    wb = wstr[k % 2]
                    dma("q", wb[:], sout_d[k * P:(k + 1) * P, :],
                        reads=("smw_out",), writes=(f"wstr{k % 2}",))
                    for m in range(2):
                        for nb in range(2):
                            OP("t", lambda h, wb=wb, k=k, m=m, nb=nb: h.matmul(
                                pb[m * 2 + nb][:],
                                xnT[:, k * LC + m * P:k * LC + (m + 1) * P],
                                wb[:, nb * 512:(nb + 1) * 512],
                                start=(k == 0), stop=(k == NDB - 1)),
                               reads=(f"wstr{k % 2}", f"xnT{k}"),
                               writes=(f"pb{m * 2 + nb}",))

        def scan_dmas(n):
            bB, bBn = bcc[n % 2], f"bcc{n % 2}"
            bC, bCn = bcc[2 + n % 2], f"bcc{2 + n % 2}"
            dma("g", bB[:], sbc_d[n:n + 1, :].broadcast_to((P, LC)),
                reads=("sbc",), writes=(bBn,))
            dma("g", bC[:], sbc_d[NST + n:NST + n + 1, :]
                .broadcast_to((P, LC)),
                reads=("sbc",), writes=(bCn,))
            return bB, bBn, bC, bCn

        def emit_scan(c, n0, n1):
            for n in range(n0, n1):
                bB, bBn, bC, bCn = scan_dmas(n)
                scan_group(c, n, 0, bB, bBn, bC, bCn)
                scan_group(c, n, 1, bB, bBn, bC, bCn)

        def emit_tail(c):
            n = NST - 1
            bB, bBn, bC, bCn = scan_dmas(n)
            scan_group(c, n, 0, bB, bBn, bC, bCn, tail=True)
            scan_group(c, n, 1, bB, bBn, bC, bCn, tail=True)
            flush_carry()

        def emit_tail2(c):
            xb = xbuf[c % 3]
            xbn = [f"xbuf{c % 3}_{i}" for i in range(2)]
            for m in range(2):
                for nb in range(2):
                    OP("v", lambda h, m=m, nb=nb: h.scalar_tensor_tensor(
                        out=xzn[m][:, nb * 512:(nb + 1) * 512],
                        in0=pb[m * 2 + nb][:], scalar=0.0,
                        in1=xzn[m][:, nb * 512:(nb + 1) * 512],
                        op0=ALU.add, op1=ALU.add),
                       reads=(f"pb{m * 2 + nb}", f"xzn{m}"),
                       writes=(f"xzn{m}",))
                OP("v", lambda h, m=m: h.tensor_tensor(
                    out=xzn[m][:], in0=xzn[m][:], in1=outb[:], op=ALU.add),
                   reads=(f"xzn{m}", "outb"), writes=(f"xzn{m}",))
            prog.drain(3)
            layernorm([xzn[m][:] for m in range(2)],
                      [f"xzn{m}" for m in range(2)],
                      [xzn[m][:] for m in range(2)],
                      [f"xzn{m}" for m in range(2)], ST1, "st1",
                      sgt[:, 0:D], ("sgt0", "sgt1", "sgt2", "sgt3"))
            prog.drain(1)
            for m in range(2):
                OP("v", lambda h, m=m: h.tensor_tensor(
                    out=xzn[m][:], in0=xzn[m][:], in1=lng[:], op=ALU.mult),
                   reads=(f"xzn{m}", "lng"), writes=(f"xzn{m}",))
                OP("v", lambda h, m=m: h.tensor_tensor(
                    out=xzn[m][:], in0=xzn[m][:], in1=lnb[:], op=ALU.add),
                   reads=(f"xzn{m}", "lnb"), writes=(f"xzn{m}",))
            prog.drain(2)
            layernorm([xzn[m][:] for m in range(2)],
                      [f"xzn{m}" for m in range(2)],
                      [xzn[m][:] for m in range(2)],
                      [f"xzn{m}" for m in range(2)], ST1, "st1",
                      sgt[:, 0:D], ("sgt0", "sgt1", "sgt2", "sgt3"))
            prog.drain(1)
            for m in range(2):
                OP("v", lambda h, m=m, xb=xb: h.tensor_tensor(
                    out=xb[m][:], in0=xb[m][:], in1=xzn[m][:], op=ALU.add),
                   reads=(xbn[m], f"xzn{m}"), writes=(xbn[m],))
            prog.drain(2)
            layernorm([xb[m][:] for m in range(2)], xbn,
                      [xzn[m][:] for m in range(2)],
                      [f"xzn{m}" for m in range(2)], ST1, "st1",
                      sgt[:, 0:D], ("sgt0", "sgt1", "sgt2", "sgt3"))
            prog.drain(1)
            for m in range(2):
                OP("v", lambda h, m=m: h.tensor_reduce(
                    out=fx_mx[:, m:m + 1], in_=xzn[m][:], axis=AX.X,
                    op=ALU.max, apply_absolute_value=True),
                   reads=(f"xzn{m}",), writes=("fx_mx",))
                OP("v", lambda h, m=m: h.tensor_scalar(
                    out=fx_mx[:, m:m + 1], in0=fx_mx[:, m:m + 1],
                    scalar1=1e-5, scalar2=None, op0=ALU.max),
                   reads=("fx_mx",), writes=("fx_mx",))
                OP("v", lambda h, m=m: h.tensor_scalar(
                    out=fx_sxi[:, m:m + 1], in0=fx_mx[:, m:m + 1],
                    scalar1=1.0 / 127.0, scalar2=None, op0=ALU.mult),
                   reads=("fx_mx",), writes=("fx_sxi",))
                OP("v", lambda h, m=m: h.reciprocal(
                    out=fx_sx[:, m:m + 1], in_=fx_sxi[:, m:m + 1]),
                   reads=("fx_sxi",), writes=("fx_sx",))
                OP("v", lambda h, m=m: h.scalar_tensor_tensor(
                    out=sc_g[:, m:m + 1], in0=fx_sxi[:, m:m + 1], scalar=0.5,
                    in1=mcg[:, 0:1], op0=ALU.mult, op1=ALU.mult),
                   reads=("fx_sxi", "mc_wg"), writes=("sc_g",))
                OP("v", lambda h, m=m: h.scalar_tensor_tensor(
                    out=sc_u[:, m:m + 1], in0=fx_sxi[:, m:m + 1], scalar=0.5,
                    in1=mcu[:, 0:1], op0=ALU.mult, op1=ALU.mult),
                   reads=("fx_sxi", "mc_wu"), writes=("sc_u",))
                for qq in range(D // LC):
                    OP("s", lambda h, m=m, qq=qq: h.activation(
                        out=pool6[0][:], in_=xzn[m][:, qq * LC:(qq + 1) * LC],
                        func=AF.Copy, scale=fx_sx[:, m:m + 1]),
                       reads=(f"xzn{m}", "fx_sx"), writes=("p6_0",))
                    round_chain(pool6[0][:], "p6_0", ffci2[:], "ffci2",
                                pool6[1][:], "p6_1", pool6[2][:], "p6_2",
                                pool6[3][:], "p6_3", rng=2)
                    bank, bknm = pb[6 + qq % 2], f"pb{6 + qq % 2}"
                    for j in range(LC // P):
                        OP("t", lambda h, j=j, bank=bank: h.transpose(
                            bank[:, j * P:(j + 1) * P],
                            pool6[3][:, j * P:(j + 1) * P], ident[:]),
                           reads=("p6_3", "ident"), writes=(bknm,))
                    OP("s", lambda h, m=m, qq=qq, bank=bank: h.activation(
                        out=xqT[:, :].rearrange("z (k il) -> z k il", il=256)
                        [:, qq * 2:qq * 2 + 2, m * P:(m + 1) * P],
                        in_=bank[:, 0:LC].rearrange(
                            "z (k il) -> z k il", il=P),
                        func=AF.Copy),
                       reads=(bknm,), writes=("xqT",))
                    prog.drain(1)

        FFNA_PC = ((swgT_d, "sc_g", sc_g,
                    [zbuf2[0], zbuf2[1]], ["zbuf2_0", "zbuf2_1"],
                    AF.Silu, "wg"),
                   (swuT_d, "sc_u", sc_u,
                    [upf[0], upf[1]], ["upf0", "upf1"],
                    AF.Copy, "wu"))

        def emit_ffnA_mms(c, pc):
            sT_d, scnm, scol, zdst, zdnm, actf, wtag = FFNA_PC[pc // 2]
            half = pc % 2
            for k in range(NDB):
                wb = wsg[k % 2]
                dma("q", wb[:], sT_d[k * P:(k + 1) * P,
                                     half * 2048:(half + 1) * 2048],
                    reads=(f"sT_{wtag}",), writes=(f"wsg{k % 2}",))
                for m2 in range(2):
                    for nb in range(4):
                        OP("t", lambda h, wb=wb, k=k, m2=m2,
                           nb=nb: h.matmul(
                               pb[m2 * 4 + nb][:],
                               xqT[:, k * 256 + m2 * P:
                                   k * 256 + (m2 + 1) * P],
                               wb[:, nb * 512:(nb + 1) * 512],
                               start=(k == 0), stop=(k == NDB - 1)),
                           reads=(f"wsg{k % 2}", "xqT"),
                           writes=(f"pb{m2 * 4 + nb}",))

        def emit_ffnA_evacs(c, pc):
            sT_d, scnm, scol, zdst, zdnm, actf, wtag = FFNA_PC[pc // 2]
            half = pc % 2
            for m2 in range(2):
                for nb in range(4):
                    off = half * 2048 + nb * 512
                    wset = (zdnm[m2],)
                    if wtag == "wg":
                        wset += (f"zbq{m2}_{off // 1024}",)
                    OP("s", lambda h, m2=m2, nb=nb, off=off,
                       zdst=zdst, scol=scol, actf=actf:
                       h.activation(
                           out=zdst[m2][:, off:off + 512],
                           in_=pb[m2 * 4 + nb][:], func=actf,
                           scale=scol[:, m2:m2 + 1]),
                       reads=(f"pb{m2 * 4 + nb}", scnm),
                       writes=wset)

        def emit_ffnA(c):
            for pc in range(4):
                emit_ffnA_mms(c, pc)
                emit_ffnA_evacs(c, pc)

        def emit_ffnB(c):
            zq_nm = [tuple(f"zbq{m2}_{q}" for q in range(4)) for m2 in range(2)]
            for m2 in range(2):
                OP("v", lambda h, m2=m2: h.tensor_tensor(
                    out=zbuf2[m2][:], in0=zbuf2[m2][:], in1=upf[m2][:],
                    op=ALU.mult),
                   reads=(f"zbuf2_{m2}",) + zq_nm[m2] + (f"upf{m2}",),
                   writes=(f"zbuf2_{m2}",) + zq_nm[m2])
                OP("v", lambda h, m2=m2: h.tensor_reduce(
                    out=fz_mx[:, m2:m2 + 1], in_=zbuf2[m2][:], axis=AX.X,
                    op=ALU.max, apply_absolute_value=True),
                   reads=(f"zbuf2_{m2}",) + zq_nm[m2], writes=("fz_mx",))
            OP("v", lambda h: h.tensor_scalar(
                out=fz_mx[:, 0:2], in0=fz_mx[:, 0:2], scalar1=1e-5,
                scalar2=None, op0=ALU.max), reads=("fz_mx",),
               writes=("fz_mx",))
            OP("v", lambda h: h.tensor_scalar(
                out=fz_szi[:, 0:2], in0=fz_mx[:, 0:2], scalar1=1.0 / 127.0,
                scalar2=None, op0=ALU.mult), reads=("fz_mx",),
               writes=("fz_szi",))
            OP("v", lambda h: h.reciprocal(
                out=fz_sz[:, 0:2], in_=fz_szi[:, 0:2]),
               reads=("fz_szi",), writes=("fz_sz",))
            for m2 in range(2):
                OP("v", lambda h, m2=m2: h.scalar_tensor_tensor(
                    out=sc_d[:, m2:m2 + 1], in0=fz_szi[:, m2:m2 + 1],
                    scalar=0.5, in1=mcd[:, 0:1], op0=ALU.mult, op1=ALU.mult),
                   reads=("fz_szi", "mc_wd"), writes=("sc_d",))
                OP("s", lambda h, m2=m2: h.activation(
                    out=zbuf2[m2][:], in_=zbuf2[m2][:], func=AF.Copy,
                    scale=fz_sz[:, m2:m2 + 1]),
                   reads=(f"zbuf2_{m2}", "fz_sz") + zq_nm[m2],
                   writes=(f"zbuf2_{m2}",) + zq_nm[m2])
                for hp in range(2):
                    pair = []
                    for hf in (hp * 2, hp * 2 + 1):
                        sl = slice(hf * 1024, (hf + 1) * 1024)
                        scrb = rscr if hf % 2 == 0 else upf[1]
                        scrn = "rscr" if hf % 2 == 0 else "upf1"
                        fsl = slice((hf % 2) * 1024, (hf % 2) * 1024 + 1024)
                        pair.append(round_chain(
                            zbuf2[m2][:, sl], f"zbq{m2}_{hf}",
                            ffci[:, fsl], f"ffci{hf % 2}",
                            scrb[:, 0:1024], scrn,
                            scrb[:, 1024:2048], scrn,
                            zbuf2[m2][:, sl], f"zbq{m2}_{hf}", rng=4,
                            collect=True))
                    for a, b in zip(*pair):
                        OP(*a)
                        OP(*b)

        def emit_ffnC_kh(c, kh):
            if True:
                for k4l in range(4):
                    k4 = kh * 4 + k4l
                    for m2 in range(2):
                        bank = pb[6 + (k4l * 2 + m2) % 2]
                        bknm = f"pb{6 + (k4l * 2 + m2) % 2}"
                        for j in range(4):
                            k = k4 * 4 + j
                            OP("t", lambda h, k=k, j=j, bank=bank,
                               m2=m2: h.transpose(
                                   bank[:, j * P:(j + 1) * P],
                                   zbuf2[m2][:, k * P:(k + 1) * P],
                                   ident[:]),
                               reads=(f"zbq{m2}_{k // 8}", "ident"),
                               writes=(bknm,))
                        OP("s", lambda h, k4l=k4l, m2=m2, bank=bank:
                           h.activation(
                               out=zqT[:, k4l * 1024 + m2 * 512:
                                       k4l * 1024 + (m2 + 1) * 512],
                               in_=bank[:, 0:512], func=AF.Copy),
                           reads=(bknm,), writes=("zqT",))
                for k2l in range(8):
                    k2 = kh * 8 + k2l
                    wb = wsd[k2 % 2]
                    dma("q", wb[:].rearrange("z (a cc) -> z a cc", a=2),
                        swdT_d[k2 * 256:(k2 + 1) * 256, :]
                        .rearrange("(a b) cc -> b a cc", b=P),
                        reads=("sT_wd",), writes=(f"wsd{k2 % 2}",))
                    for j in range(2):
                        k = k2 * 2 + j
                        kl = k - kh * 16
                        k4l, jj = kl // 4, kl % 4
                        for m2 in range(2):
                            col = k4l * 1024 + m2 * 512 + jj * P
                            for nb in range(2):
                                OP("t", lambda h, wb=wb, k=k, j=j, m2=m2,
                                   nb=nb, col=col: h.matmul(
                                       pb[m2 * 2 + nb][:],
                                       zqT[:, col:col + P],
                                       wb[:, j * 1024 + nb * 512:
                                          j * 1024 + (nb + 1) * 512],
                                       start=(k == 0), stop=(k == 31)),
                                   reads=(f"wsd{k2 % 2}", "zqT"),
                                   writes=(f"pb{m2 * 2 + nb}",))
        def emit_ffnC_fin(c):
            xb = xbuf[c % 3]
            xbn = [f"xbuf{c % 3}_{i}" for i in range(2)]
            for m2 in range(2):
                for nb in range(2):
                    OP("s", lambda h, m2=m2, nb=nb: h.activation(
                        out=upf[0][:, m2 * 2048 + nb * 512:
                                   m2 * 2048 + (nb + 1) * 512],
                        in_=pb[m2 * 2 + nb][:], func=AF.Copy,
                        scale=sc_d[:, m2:m2 + 1]),
                       reads=(f"pb{m2 * 2 + nb}", "sc_d"),
                       writes=("upf0",))
            layernorm([upf[0][:, m2 * 2048:m2 * 2048 + D] for m2 in range(2)],
                      ["upf0", "upf0"],
                      [upf[0][:, m2 * 2048:m2 * 2048 + D] for m2 in range(2)],
                      ["upf0", "upf0"], ST2, "st2", upf[1][:, 0:D], "upf1")
            for m2 in range(2):
                OP("v", lambda h, m2=m2, xb=xb: h.tensor_tensor(
                    out=xb[m2][:], in0=xb[m2][:],
                    in1=upf[0][:, m2 * 2048:m2 * 2048 + D], op=ALU.add),
                   reads=(xbn[m2], "upf0"), writes=(xbn[m2],))
                dma("q", out_d[c * LC + m2 * P:c * LC + (m2 + 1) * P, :],
                    xb[m2][:], reads=(xbn[m2],), writes=(f"out{c}_{m2}",))

        emit_front_p1(0)
        emit_front_p2(0)
        for c in range(NCH):
            emit_scan(c, 0, 3)
            if c > 1:
                emit_ffnC_kh(c - 2, 0)
            emit_scan(c, 3, 6)
            if c > 1:
                emit_ffnC_kh(c - 2, 1)
            emit_scan(c, 6, 8)
            if c > 1:
                emit_ffnC_fin(c - 2)
            if c > 0:
                emit_ffnA_mms(c - 1, 0)
            emit_scan(c, 8, 10)
            if c > 0:
                emit_ffnA_evacs(c - 1, 0)
                emit_ffnA_mms(c - 1, 1)
            emit_scan(c, 10, 12)
            if c > 0:
                emit_ffnA_evacs(c - 1, 1)
                emit_ffnA_mms(c - 1, 2)
            emit_scan(c, 12, 14)
            if c > 0:
                emit_ffnA_evacs(c - 1, 2)
                emit_ffnA_mms(c - 1, 3)
            emit_scan(c, 14, NST - 1)
            if c > 0:
                emit_ffnA_evacs(c - 1, 3)
            emit_tail(c)
            if c > 0:
                prog.stashing = True
                emit_ffnB(c - 1)
                prog.stashing = False
            emit_tail2(c)
            if c + 1 < NCH:
                emit_front_p1(c + 1)
            if c + 1 < NCH:
                emit_front_p2(c + 1)
            prog.drain()
        emit_ffnC_kh(NCH - 2, 0)
        emit_ffnC_kh(NCH - 2, 1)
        emit_ffnC_fin(NCH - 2)
        emit_ffnA(NCH - 1)
        emit_ffnB(NCH - 1)
        emit_ffnC_kh(NCH - 1, 0)
        emit_ffnC_kh(NCH - 1, 1)
        emit_ffnC_fin(NCH - 1)

        block = es.enter_context(nc.Block())
        block.tensor(prog.runner("t"))
        block.vector(prog.runner("v"))
        block.scalar(prog.runner("s"))
        block.sync(prog.runner("q"))
        block.gpsimd(prog.runner("g"))

    return nc


def kernel(L0_attn_x, L0_attn_in_proj_w, L0_attn_in_proj_b, L0_attn_conv_w,
           L0_attn_conv_b, L0_attn_bc_w, L0_attn_bc_b, L0_attn_dt_w,
           L0_attn_dt_b, L0_attn_A_log, L0_attn_D, L0_attn_out_w,
           L0_attn_out_b, L0_attn_ln_g, L0_attn_ln_b, L0_ffn_w_gate,
           L0_ffn_w_up, L0_ffn_w_down):
    f32 = np.float32
    c = np.ascontiguousarray

    x = c(np.asarray(L0_attn_x, dtype=f32))
    ipwT = c(np.asarray(L0_attn_in_proj_w, dtype=f32).T)
    ipb_c = c(np.asarray(L0_attn_in_proj_b, dtype=f32).reshape(16, P).T)
    convw_c = c(np.asarray(L0_attn_conv_w, dtype=f32)[:, 0, :]
                .reshape(NDB, P, 4).transpose(1, 0, 2).reshape(P, 32))
    convb_c = c(np.asarray(L0_attn_conv_b, dtype=f32).reshape(NDB, P).T)
    bcwT = c(np.asarray(L0_attn_bc_w, dtype=f32).T)
    bcb_c = c(np.asarray(L0_attn_bc_b, dtype=f32).reshape(32, 1))
    dtwT = c(np.asarray(L0_attn_dt_w, dtype=f32).T)
    dtb_c = c(np.asarray(L0_attn_dt_b, dtype=f32).reshape(NDB, P).T)
    alog_c = c(np.asarray(L0_attn_A_log, dtype=f32)
               .reshape(NDB, P, NST).transpose(1, 0, 2).reshape(P, P))
    d_c = c(np.asarray(L0_attn_D, dtype=f32).reshape(NDB, P).T)
    outwT = c(np.asarray(L0_attn_out_w, dtype=f32).T)
    outb_r = c(np.tile(np.asarray(L0_attn_out_b, dtype=f32)[None, :], (P, 1)))
    lng_r = c(np.tile(np.asarray(L0_attn_ln_g, dtype=f32)[None, :], (P, 1)))
    lnb_r = c(np.tile(np.asarray(L0_attn_ln_b, dtype=f32)[None, :], (P, 1)))
    wg = c(np.asarray(L0_ffn_w_gate, dtype=f32))
    wu = c(np.asarray(L0_ffn_w_up, dtype=f32))
    wd = c(np.asarray(L0_ffn_w_down, dtype=f32))
    ident = np.eye(P, dtype=f32)
    onescol = np.ones((P, 1), dtype=f32)
    consts = np.zeros((P, 8), dtype=f32)
    consts[:, C_EPS] = EPS
    consts[:, C_NHALF] = -0.5
    consts[:, C_PHALF] = 0.5
    consts[:, C_NHALFSC] = -0.5
    consts[:, C_127] = 127.0

    nc = build_nc()
    shared = {"ipwT": ipwT, "ipb_c": ipb_c, "convw_c": convw_c,
              "convb_c": convb_c, "bcwT": bcwT, "bcb_c": bcb_c, "dtwT": dtwT,
              "dtb_c": dtb_c, "alog_c": alog_c, "d_c": d_c, "outwT": outwT,
              "outb_r": outb_r, "lng_r": lng_r, "lnb_r": lnb_r, "wg": wg,
              "wu": wu, "wd": wd, "ident": ident, "onescol": onescol,
              "consts": consts}
    in_maps = [dict(shared, x=c(x[b])) for b in range(8)]
    res = run_bass_kernel_spmd(nc, in_maps, core_ids=list(range(8)))
    return np.stack([np.asarray(r["out"], dtype=f32) for r in res.results])
